# version 12
# baseline (speedup 1.0000x reference)
import math
from contextlib import ExitStack

import numpy as np

import concourse.bass as bass
import concourse.mybir as mybir
from concourse.bass_utils import run_bass_kernel_spmd

F32 = mybir.dt.float32
BF16 = mybir.dt.bfloat16
I32 = mybir.dt.int32
AF = mybir.ActivationFunctionType
OP = mybir.AluOpType
AX = mybir.AxisListType

NSEQ = 4
T = 2048
D = 1024
NCH = 8
CW = T + 1
MLA_HEADS = 8
MLA_KV = 2
MLA_DH = 128
MLA_LAT = 32
MLA_SCALE = float(MLA_DH) ** -0.5
L_HEADS = 8
L_DH = 64
WINDOW = 512
NWIN = T // WINDOW
INNER = L_HEADS * L_DH
HID = 2730
QK_SCALE = 8.0
NEG = -1.0e30

TWO_PI = 2.0 * math.pi
PI_F32 = float(np.float32(math.pi))
CW_C1 = float(np.float32(6.28125))
CW_C2 = float(np.float32(TWO_PI - CW_C1))
CW_C3 = float(np.float64(TWO_PI) - CW_C1 - np.float64(np.float32(TWO_PI - CW_C1)))

ENGS = ("sync", "tensor", "vector", "scalar", "gpsimd")


class Buf:
    __slots__ = ("ap", "w", "rs")

    def __init__(self, ap):
        self.ap = ap
        self.w = None
        self.rs = []


class Prog:

    def __init__(self, nc):
        self.nc = nc
        self.steps = {e: [] for e in ENGS}
        self.cnt = {e: 0 for e in ENGS}
        self.dma_cnt = {}
        self.dma_sems = []

    def dma_sem(self, name):
        if name not in self.dma_cnt:
            self.dma_cnt[name] = 0
            self.dma_sems.append(name)
        return name

    def em(self, eng, fn, deps=(), inc=1, dsem=None):
        deps = [d for d in deps if d is not None]
        for (pe, pv) in deps:
            lim = self.cnt[pe] if pe in self.cnt else self.dma_cnt[pe]
            assert pv <= lim, f"dep on future: {pe} {pv} > {lim}"
        self.steps[eng].append((deps, fn, inc, dsem))
        if dsem is None:
            self.cnt[eng] += inc
            return (eng, self.cnt[eng])
        else:
            self.dma_cnt[dsem] += inc
            return (dsem, self.dma_cnt[dsem])

    def step(self, eng, fn, reads=(), writes=(), inc=1, extra_deps=(), dsem=None,
             reads_reg=()):
        deps = list(extra_deps)
        for b in reads:
            if b.w is not None:
                deps.append(b.w)
        for b in writes:
            if b.w is not None:
                deps.append(b.w)
            deps.extend(b.rs)
        tok = self.em(eng, fn, deps, inc, dsem)
        for b in reads:
            b.rs.append(tok)
        for b in reads_reg:
            b.rs.append(tok)
        for b in writes:
            b.w = tok
            b.rs = []
        return tok

    @staticmethod
    def merge(tokens):
        best = {}
        for t in tokens:
            if t is None:
                continue
            if t[1] > best.get(t[0], 0):
                best[t[0]] = t[1]
        return [(k, v) for k, v in best.items()]

    def barrier(self):
        self.dma_sem("BGAT")
        self.dma_sem("BREL")
        dtoks = [(s, self.dma_cnt[s]) for s in self.dma_sems
                 if self.dma_cnt[s] > 0 and s not in ("BGAT", "BREL")]
        for e in ENGS:
            self.em(e, "DRAIN", [], inc=1, dsem="BGAT")
        gat_full = ("BGAT", self.dma_cnt["BGAT"])
        self.em("gpsimd", None, [gat_full] + dtoks, inc=1, dsem="BREL")
        rel = ("BREL", self.dma_cnt["BREL"])
        for e in ENGS:
            if e != "gpsimd":
                self.em(e, None, [rel], inc=0)

    def build(self):
        nc = self.nc
        with ExitStack() as stk:
            sems = {e: stk.enter_context(nc.semaphore(f"S_{e}")) for e in ENGS}
            for s in self.dma_sems:
                sems[s] = stk.enter_context(nc.semaphore(f"D_{s}"))
            with nc.Block() as block:
                for e in ENGS:
                    def mk(e):
                        def body(eng):
                            hw = {}
                            for deps, fn, inc, dsem in self.steps[e]:
                                for (pe, pv) in deps:
                                    if pv > hw.get(pe, 0):
                                        eng.wait_ge(sems[pe], pv)
                                        hw[pe] = pv
                                if fn == "DRAIN":
                                    inst = eng.drain()
                                elif fn is not None:
                                    inst = fn(eng)
                                else:
                                    inst = eng.nop()
                                if inc:
                                    inst.then_inc(sems[e if dsem is None else dsem], inc)
                        return body
                    getattr(block, e)(mk(e))


def ceil_div(a, b):
    return (a + b - 1) // b



def build_program(debug_taps=None):
    nc = bass.Bass("TRN2", target_bir_lowering=False)
    P = Prog(nc)

    x_in = nc.dram_tensor("x", [NSEQ, T, D], F32, kind="ExternalInput").ap()
    out_t = nc.dram_tensor("out", [NSEQ, T, D], F32, kind="ExternalOutput").ap()
    W = {}
    for name, shape in [
        ("attn_wq", [D, 1024]), ("attn_wk_down", [D, 64]), ("attn_wv_down", [D, 64]),
        ("attn_wk_up", [32, 128]), ("attn_wv_up", [32, 128]), ("attn_wo", [1024, D]),
        ("ffn_wqkv", [2, D, 1536]), ("ffn_q_scale", [2, 64]), ("ffn_k_scale", [2, 64]),
        ("ffn_wo", [2, 512, D]), ("ffn_ln_g", [2, D]), ("ffn_ln_b", [2, D]),
        ("ffn_w1", [2, D, 5460]), ("ffn_b1", [2, 5460]), ("ffn_w2", [2, HID, D]),
        ("ffn_b2", [2, D]),
        ("ffn_qsr", [2, 128, 1]), ("ffn_ksr", [2, 128, 1]),
        ("ffn_b1ar", [2, 128, 22]), ("ffn_b1gr", [2, 128, 22]),
        ("ffn_lngr", [2, 128, 8]), ("ffn_lnbr", [2, 128, 8]), ("ffn_b2r", [2, 128, 8]),
        ("attn_wkdr", [128, 2048]), ("attn_wvdr", [128, 512]),
    ]:
        W[name] = nc.dram_tensor(name, shape, F32, kind="ExternalInput").ap()
    identc = nc.dram_tensor("identc", [128, 128], F32, kind="ExternalInput").ap()
    protT = nc.dram_tensor("protT", [128, 128], F32, kind="ExternalInput").ap()
    tri_up = nc.dram_tensor("tri_up", [128, 128], F32, kind="ExternalInput").ap()
    tri_lo = nc.dram_tensor("tri_lo", [128, 128], F32, kind="ExternalInput").ap()
    ind2c = nc.dram_tensor("ind2c", [2, 128], F32, kind="ExternalInput").ap()
    tabc = nc.dram_tensor("tabc", [4, 128, 1], F32, kind="ExternalInput").ap()

    park_xn = nc.dram_tensor("park_xn", [NSEQ, 128, NCH * CW], F32).ap()
    park_tab = nc.dram_tensor("park_tab", [4, 128, T], F32).ap()
    park_ab = nc.dram_tensor("park_ab", [2, 128, 2048], F32).ap()
    w1s = nc.dram_tensor("w1s", [2, 2, D, 5460], BF16).ap()
    w2s = nc.dram_tensor("w2s", [2, 2, HID, D], BF16).ap()
    wqkvs = nc.dram_tensor("wqkvs", [2, 2, D, 1536], BF16).ap()
    wofs = nc.dram_tensor("wofs", [2, 2, 512, D], BF16).ap()

    env = {}
    taps = dict(debug_taps or [])

    with ExitStack() as G:
        def sb(name, p, f, dt=F32):
            return G.enter_context(nc.sbuf_tensor(name, [p, f], dt))

        ident = Buf(sb("ident", 128, 128).ap())
        prot = Buf(sb("prot", 128, 128).ap())
        triu = Buf(sb("triu", 128, 128).ap())
        tril = Buf(sb("tril", 128, 128).ap())
        onesm = Buf(sb("onesm", 128, 128).ap())
        h2 = Buf(sb("h2", 128, 2).ap())
        ind2 = Buf(sb("ind2", 2, 128).ap())
        stream_t = sb("stream", 128, NCH * CW)
        stream = [Buf(stream_t[:, c * CW: (c + 1) * CW]) for c in range(NCH)]
        env["stream"] = stream
        env["stream_t"] = stream_t

        csem = P.dma_sem("consts")
        P.em("sync", lambda e: e.dma_start(out=ident.ap, in_=identc[:, :]), [], 16, csem)
        P.em("sync", lambda e: e.dma_start(out=prot.ap, in_=protT[:, :]), [], 16, csem)
        P.em("sync", lambda e: e.dma_start(out=triu.ap, in_=tri_up[:, :]), [], 16, csem)
        P.em("sync", lambda e: e.dma_start(out=tril.ap, in_=tri_lo[:, :]), [], 16, csem)
        P.em("sync", lambda e: e.dma_start(out=ind2.ap, in_=ind2c[:, :]), [], 16, csem)
        for b in (ident, prot, triu, tril, ind2):
            b.w = (csem, 80)
        P.step("vector", lambda e: e.memset(onesm.ap, 1.0), writes=[onesm])
        P.step("vector", lambda e: e.memset(h2.ap, 0.0), writes=[h2])
        P.step("vector", lambda e: e.memset(h2.ap[0:64, 0:1], 1.0), reads=[h2], writes=[h2])
        P.step("vector", lambda e: e.memset(h2.ap[64:128, 1:2], 1.0), reads=[h2], writes=[h2])

        env.update(ident=ident, prot=prot, triu=triu, tril=tril, onesm=onesm,
                   h2=h2, ind2=ind2)

        env["tab_consts_dram"] = tabc
        with ExitStack() as TG:
            build_tables(nc, P, TG, env, park_tab)
            if "tables" in taps:
                taps["tables"](nc, P, env)
            P.barrier()

        precompute_AB(nc, P, G, env, W, park_ab)

        split_weights(nc, P, G, env, W, w1s, w2s, wqkvs, wofs)
        env["w1s"], env["w2s"], env["wqkvs"] = w1s, w2s, wqkvs
        env["wofs"] = wofs

        env["tapBattn"] = taps.get("Battn")
        for s in range(NSEQ):
            phase_A(nc, P, G, env, x_in, park_xn, s)
            if s == 0 and "A" in taps:
                taps["A"](nc, P, env)
            P.barrier()
            for l in range(2):
                phase_B_layer(nc, P, G, env, W, park_tab, s, l)
                if s == 0 and f"B{l}" in taps:
                    taps[f"B{l}"](nc, P, env)
                P.barrier()
            phase_C(nc, P, G, env, W, park_xn, park_ab, s)
            if s == 0 and "C" in taps:
                taps["C"](nc, P, env)
            P.barrier()
            phase_D(nc, P, G, env, x_in, out_t, s)
            P.barrier()

        P.barrier()

    P.build()
    return nc, env


def build_tables(nc, P, G, env, park_tab):
    ts = lambda n, p, f, dt=F32: G.enter_context(nc.sbuf_tensor(n, [p, f], dt))
    invfh = Buf(ts("tb_invfh", 128, 1).ap())
    invfl = Buf(ts("tb_invfl", 128, 1).ap())
    lnbh = Buf(ts("tb_lnbh", 128, 1).ap())
    lnbl = Buf(ts("tb_lnbl", 128, 1).ap())
    trow = Buf(ts("tb_trow", 128, T).ap())
    freq = Buf(ts("tb_freq", 128, T).ap())
    kf = Buf(ts("tb_kf", 128, T).ap())
    ki = Buf(ts("tb_ki", 128, T, I32).ap())
    r = Buf(ts("tb_r", 128, T).ap())
    cmp = Buf(ts("tb_cmp", 128, T).ap())
    rw = Buf(ts("tb_rw", 128, T).ap())
    sinr = Buf(ts("tb_sin", 128, T).ap())
    cosr = Buf(ts("tb_cos", 128, T).ap())
    scl = Buf(ts("tb_scl", 128, T).ap())
    sclin = Buf(ts("tb_sclin", 128, T).ap())
    tabs = Buf(ts("tb_tab", 128, T).ap())
    env["tab_bufs"] = (sinr, cosr, scl, sclin)
    env["ptab_sem"] = P.dma_sem("ptab")

    V, S, GP = "vector", "scalar", "gpsimd"
    tcsem = P.dma_sem("tabconsts")
    tc = env["tab_consts_dram"]
    for i, b in enumerate((invfh, invfl, lnbh, lnbl)):
        P.em("sync", lambda e, i=i, b=b: e.dma_start(out=b.ap, in_=tc[i]), [], 16, tcsem)
    for b in (invfh, invfl, lnbh, lnbl):
        b.w = (tcsem, 64)
    P.step(GP, lambda e: e.iota(ki.ap, [[1, T]], base=0, channel_multiplier=0),
           reads=[], writes=[ki])
    P.step(V, lambda e: e.tensor_copy(trow.ap, ki.ap), reads=[ki], writes=[trow])
    P.step(V, lambda e: e.tensor_scalar(freq.ap, trow.ap, invfh.ap, None, op0=OP.mult),
           reads=[trow, invfh], writes=[freq])
    P.step(V, lambda e: e.tensor_scalar(cmp.ap, trow.ap, invfl.ap, None, op0=OP.mult),
           reads=[trow, invfl], writes=[cmp])
    P.step(V, lambda e: e.tensor_tensor(freq.ap, freq.ap, cmp.ap, op=OP.add),
           reads=[freq, cmp], writes=[freq])
    P.step(V, lambda e: e.tensor_scalar(kf.ap, freq.ap, 1.0 / TWO_PI, None, op0=OP.mult),
           reads=[freq], writes=[kf])
    P.step(V, lambda e: e.tensor_copy(ki.ap, kf.ap), reads=[kf], writes=[ki])
    P.step(V, lambda e: e.tensor_scalar(kf.ap, ki.ap, -1.0, None, op0=OP.mult),
           reads=[ki], writes=[kf])
    for c in (CW_C1, CW_C2, CW_C3):
        P.step(V, lambda e, c=c: e.scalar_tensor_tensor(
            r.ap, kf.ap, c, freq.ap if c == CW_C1 else r.ap, OP.mult, OP.add),
            reads=[kf, freq, r] if c != CW_C1 else [kf, freq], writes=[r])
    P.step(V, lambda e: e.tensor_scalar(cmp.ap, r.ap, PI_F32, None, op0=OP.is_gt),
           reads=[r], writes=[cmp])
    P.step(V, lambda e: e.scalar_tensor_tensor(r.ap, cmp.ap, -TWO_PI, r.ap, OP.mult, OP.add),
           reads=[cmp, r], writes=[r])
    P.step(V, lambda e: e.tensor_scalar(cmp.ap, r.ap, -PI_F32, None, op0=OP.is_lt),
           reads=[r], writes=[cmp])
    P.step(V, lambda e: e.scalar_tensor_tensor(r.ap, cmp.ap, TWO_PI, r.ap, OP.mult, OP.add),
           reads=[cmp, r], writes=[r])
    P.step(V, lambda e: e.tensor_scalar(r.ap, r.ap, PI_F32, -PI_F32, op0=OP.min, op1=OP.max),
           reads=[r], writes=[r])
    P.step(V, lambda e: e.tensor_scalar(rw.ap, r.ap, math.pi / 2, None, op0=OP.add),
           reads=[r], writes=[rw])
    P.step(V, lambda e: e.tensor_scalar(cmp.ap, rw.ap, PI_F32, None, op0=OP.is_gt),
           reads=[rw], writes=[cmp])
    P.step(V, lambda e: e.scalar_tensor_tensor(rw.ap, cmp.ap, -TWO_PI, rw.ap, OP.mult, OP.add),
           reads=[cmp, rw], writes=[rw])
    P.step(V, lambda e: e.tensor_scalar(rw.ap, rw.ap, PI_F32, -PI_F32, op0=OP.min, op1=OP.max),
           reads=[rw], writes=[rw])
    P.step(S, lambda e: e.activation(out=sinr.ap, in_=r.ap, func=AF.Sin),
           reads=[r], writes=[sinr])
    P.step(S, lambda e: e.activation(out=cosr.ap, in_=rw.ap, func=AF.Sin),
           reads=[rw], writes=[cosr])
    P.step(V, lambda e: e.tensor_scalar(cmp.ap, trow.ap, 1.0 / 256.0, -4.0,
                                        op0=OP.mult, op1=OP.add),
           reads=[trow], writes=[cmp])
    P.step(V, lambda e: e.tensor_scalar(scl.ap, cmp.ap, lnbh.ap, None, op0=OP.mult),
           reads=[cmp, lnbh], writes=[scl])
    P.step(V, lambda e: e.tensor_scalar(sclin.ap, cmp.ap, lnbl.ap, None, op0=OP.mult),
           reads=[cmp, lnbl], writes=[sclin])
    P.step(V, lambda e: e.tensor_tensor(scl.ap, scl.ap, sclin.ap, op=OP.add),
           reads=[scl, sclin], writes=[scl])
    P.step(V, lambda e: e.tensor_scalar(sclin.ap, scl.ap, -1.0, None, op0=OP.mult),
           reads=[scl], writes=[sclin])
    P.step(S, lambda e: e.activation(out=scl.ap, in_=scl.ap, func=AF.Exp),
           reads=[scl], writes=[scl])
    P.step(S, lambda e: e.activation(out=sclin.ap, in_=sclin.ap, func=AF.Exp),
           reads=[sclin], writes=[sclin])
    for i, (trig, sc) in enumerate([(cosr, scl), (sinr, scl), (cosr, sclin), (sinr, sclin)]):
        P.step(V, lambda e, trig=trig, sc=sc: e.tensor_tensor(tabs.ap, trig.ap, sc.ap,
                                                              op=OP.mult),
               reads=[trig, sc], writes=[tabs])
        P.step(GP, lambda e, i=i: e.dma_start(out=park_tab[i], in_=tabs.ap),
               reads=[tabs], writes=[], inc=16, dsem=env["ptab_sem"])


def emit_l2norm_rotary(nc, P, env, psum_in, tgt_ap, tgt_buf, scale_vec, tabc, tabs,
                       aux, pfx):
    h2, ind2, prot = env["h2"], env["ind2"], env["prot"]
    sq, nrm, bc_sb, qn, u = aux["sq"], aux["nrm"], aux["bc_sb"], aux["qn"], aux["u"]
    p_ssq, p_bc, p_rot = aux["p_ssq"], aux["p_bc"], aux["p_rot"]
    P.step("scalar", lambda e: e.activation(out=sq.ap, in_=psum_in.ap, func=AF.Square),
           reads=[psum_in], writes=[sq])
    P.step("tensor", lambda e: e.matmul(p_ssq.ap, h2.ap, sq.ap, start=True, stop=True),
           reads=[h2, sq], writes=[p_ssq])
    P.step("scalar", lambda e: e.activation(out=nrm.ap, in_=p_ssq.ap, func=AF.Sqrt),
           reads=[p_ssq], writes=[nrm])
    P.step("vector", lambda e: e.tensor_scalar(nrm.ap, nrm.ap, 1e-12, None, op0=OP.max),
           reads=[nrm], writes=[nrm])
    P.step("vector", lambda e: e.reciprocal(nrm.ap, nrm.ap), reads=[nrm], writes=[nrm])
    P.step("tensor", lambda e: e.matmul(p_bc.ap, ind2.ap, nrm.ap, start=True, stop=True),
           reads=[ind2, nrm], writes=[p_bc])
    P.step("vector", lambda e: e.tensor_copy(bc_sb.ap, p_bc.ap), reads=[p_bc], writes=[bc_sb])
    P.step("vector", lambda e: e.scalar_tensor_tensor(qn.ap, psum_in.ap, scale_vec.ap,
                                                      bc_sb.ap, OP.mult, OP.mult),
           reads=[psum_in, scale_vec, bc_sb], writes=[qn])
    P.step("tensor", lambda e: e.matmul(p_rot.ap, prot.ap, qn.ap, start=True, stop=True),
           reads=[prot, qn], writes=[p_rot])
    P.step("vector", lambda e: e.tensor_tensor(u.ap, qn.ap, tabc, op=OP.mult),
           reads=[qn] + aux["tab_reads"], writes=[u])
    P.step("vector", lambda e: e.tensor_tensor(qn.ap, p_rot.ap, tabs, op=OP.mult),
           reads=[p_rot] + aux["tab_reads"], writes=[qn])
    P.step("vector", lambda e: e.tensor_tensor(tgt_ap, u.ap, qn.ap, op=OP.add),
           reads=[u, qn], writes=[tgt_buf])


def phase_B_layer(nc, P, G, env, W, park_tab, s, l):
    stream = env["stream"]
    onesm, triu, tril, ident = env["onesm"], env["triu"], env["tril"], env["ident"]
    wqkv = W["ffn_wqkv"][l].rearrange("(a p) n -> p a n", p=128)
    w1r = W["ffn_w1"][l].rearrange("(a p) n -> p a n", p=128)
    w2r = W["ffn_w2"][l]
    wor = W["ffn_wo"][l]

    def mm_acc(pb, items, region=None):
        fd = ([pb.w] if pb.w else []) + pb.rs
        for _, _, rd, _ in items:
            fd += [b.w for b in rd]
        fd = Prog.merge(fd)
        n = len(items)
        for i, (lhs, rhs, rd, osl) in enumerate(items):
            out = pb.ap if osl is None else osl
            P.step("tensor", lambda e, lhs=lhs, rhs=rhs, out=out, st=(i == 0),
                   sp=(i == n - 1): e.matmul(out, lhs, rhs, start=st, stop=sp),
                   reads_reg=rd, writes=[], extra_deps=fd if i == 0 else [])
        pb.w = ("tensor", P.cnt["tensor"])
        pb.rs = []

    with ExitStack() as LA:
        ts = lambda n, p, f, dt=F32: LA.enter_context(
            nc.sbuf_tensor(f"B{s}{l}_{n}", [p, f], dt))
        pool = [Buf(LA.enter_context(
            nc.psum_tensor(f"B{s}{l}_pp{i}", [128, 512], F32)).ap()) for i in range(8)]
        tabs4 = [Buf(ts(f"tab{i}", 128, 512).ap()) for i in range(4)]
        tbsem = [P.dma_sem(f"Btab{i}") for i in range(4)]
        ptok = (env["ptab_sem"], P.dma_cnt[env["ptab_sem"]])

        def load_tables(w):
            for i in range(4):
                P.step("sync", lambda e, i=i, w=w: e.dma_start(
                    out=tabs4[i].ap, in_=park_tab[i][:, 512 * w:512 * (w + 1)]),
                    reads=[], writes=[tabs4[i]], inc=16, dsem=tbsem[i],
                    extra_deps=[ptok])
        vsem = P.dma_sem("Bvec")
        qsv = Buf(ts("qsv", 128, 1).ap())
        ksv = Buf(ts("ksv", 128, 1).ap())
        P.step("sync", lambda e: e.dma_start(out=qsv.ap, in_=W["ffn_qsr"][l]),
               reads=[], writes=[qsv], inc=16, dsem=vsem)
        P.step("sync", lambda e: e.dma_start(out=ksv.ap, in_=W["ffn_ksr"][l]),
               reads=[], writes=[ksv], inc=16, dsem=vsem)
        vtok = (vsem, P.dma_cnt[vsem])
        qsv.w = vtok
        ksv.w = vtok

        qf = Buf(ts("qf", 128, 4 * 512).ap())
        kf = Buf(ts("kf", 128, 8 * 512).ap())
        vt = Buf(ts("vt", 128, 8 * 520).ap())
        probsT = Buf(ts("probsT", 128, 8 * 512).ap())
        y_w = Buf(ts("y_w", 64, 512).ap())
        aux2 = [{
            "sq": Buf(ts(f"sq{p}", 128, 512).ap()),
            "nrm": Buf(ts(f"nrm{p}", 2, 512).ap()),
            "bc_sb": Buf(ts(f"bc_sb{p}", 128, 512).ap()),
            "qn": Buf(ts(f"qn{p}", 128, 512).ap()),
            "u": Buf(ts(f"u{p}", 128, 512).ap()),
        } for p in range(2)]
        rec_t = ts("rec", 128, 512)
        rec = Buf(rec_t.ap()[64:65, :])
        ybc = Buf(ts("ybc", 64, 512).ap())
        tsb = lambda n, p, f: LA.enter_context(
            nc.sbuf_tensor(f"B{s}{l}_{n}", [p, f], BF16))
        wsem = [P.dma_sem(f"Bwq{i}") for i in range(4)]
        wqb = [(Buf(tsb(f"wqbh{i}", 128, 8 * 128).ap()),
                Buf(tsb(f"wqbl{i}", 128, 8 * 128).ap())) for i in range(2)]
        hh_b = Buf(tsb("hh", 128, 8 * 512).ap())
        hl_b = Buf(tsb("hl", 128, 8 * 512).ap())
        htmp = Buf(ts("htmp", 128, 512).ap())
        wqv = [env["wqkvs"][l][i].rearrange("(a p) n -> p a n", p=128) for i in range(2)]
        wosem = [P.dma_sem(f"Bwo{i}") for i in range(4)]
        wob = [(Buf(tsb(f"wobh{i}", 64, 1024).ap()),
                Buf(tsb(f"wobl{i}", 64, 1024).ap())) for i in range(2)]
        ywh = Buf(tsb("ywh", 64, 8 * 512).ap())
        ywl = Buf(tsb("ywl", 64, 8 * 512).ap())
        wofr = env["wofs"][l]
        vt_view = vt.ap.rearrange("p (sl r) -> p sl r", r=520)
        ones_cols = vt.ap.rearrange("p (sl h c) -> p (sl h) c", h=8, c=65)[:, :, 64:65]
        P.step("vector", lambda e: e.memset(ones_cols, 1.0), reads=[], writes=[vt])

        def qkv_window(w):
            sl = w % 2
            load_tables(w)
            for dc in range(NCH):
                shift = 1 if dc < 4 else 0
                h_ap = stream[dc].ap[:, shift + 512 * w: shift + 512 * (w + 1)]
                P.step("vector", lambda e, h_ap=h_ap, dc=dc: e.tensor_copy(
                    hh_b.ap[:, 512 * dc:512 * (dc + 1)], h_ap),
                    reads=[stream[dc]], writes=[hh_b])
                P.step("vector", lambda e, h_ap=h_ap, dc=dc: e.tensor_tensor(
                    hl_b.ap[:, 512 * dc:512 * (dc + 1)], h_ap,
                    hh_b.ap[:, 512 * dc:512 * (dc + 1)], op=OP.subtract),
                    reads=[stream[dc], hh_b], writes=[hl_b])
            for oc in range(12):
                wbh, wbl = wqb[oc % 2]
                P.step("sync", lambda e, oc=oc, wbh=wbh: e.dma_start(
                    out=wbh.ap.rearrange("p (a n) -> p a n", n=128),
                    in_=wqv[0][:, :, 128 * oc:128 * (oc + 1)]),
                    reads=[], writes=[wbh], inc=16, dsem=wsem[oc % 2])
                P.step("sync", lambda e, oc=oc, wbl=wbl: e.dma_start(
                    out=wbl.ap.rearrange("p (a n) -> p a n", n=128),
                    in_=wqv[1][:, :, 128 * oc:128 * (oc + 1)]),
                    reads=[], writes=[wbl], inc=16, dsem=wsem[2 + oc % 2])
                pq = pool[oc % 2]
                items = []
                for dc in range(NCH):
                    wh = wbh.ap[:, 128 * dc:128 * (dc + 1)]
                    wl = wbl.ap[:, 128 * dc:128 * (dc + 1)]
                    xh = hh_b.ap[:, 512 * dc:512 * (dc + 1)]
                    xl = hl_b.ap[:, 512 * dc:512 * (dc + 1)]
                    items.append((wh, xh, [wbh, hh_b], None))
                    items.append((wh, xl, [wbh, hl_b], None))
                    items.append((wl, xh, [wbl, hh_b], None))
                mm_acc(pq, items)
                if oc < 8:
                    pr = oc % 4
                    is_q = oc < 4
                    sv = qsv if is_q else ksv
                    tc = tabs4[0] if is_q else tabs4[2]
                    tsn = tabs4[1] if is_q else tabs4[3]
                    if is_q:
                        tgt_ap = qf.ap[:, 512 * pr: 512 * (pr + 1)]
                        tgt_buf = qf
                    else:
                        tgt_ap = kf.ap[:, (pr * 2 + sl) * 512: (pr * 2 + sl + 1) * 512]
                        tgt_buf = kf
                    par = oc % 2
                    p_ssq, p_bc, p_rot = pool[2 + 3 * par], pool[3 + 3 * par], pool[4 + 3 * par]
                    aux = aux2[par]
                    sq, nrm, bc_sb, qn, u = (aux["sq"], aux["nrm"], aux["bc_sb"],
                                             aux["qn"], aux["u"])
                    h2, ind2, prot = env["h2"], env["ind2"], env["prot"]
                    P.step("scalar", lambda e, pq=pq, sq=sq: e.activation(
                        out=sq.ap, in_=pq.ap, func=AF.Square),
                        reads=[pq], writes=[sq])
                    mm_acc(p_ssq, [(h2.ap, sq.ap, [h2, sq], p_ssq.ap[0:2, :])])
                    P.step("vector", lambda e, nrm=nrm, p_ssq=p_ssq: e.tensor_scalar(
                        nrm.ap, p_ssq.ap[0:2, :], 1e-24, None, op0=OP.max),
                        reads=[p_ssq, nrm], writes=[nrm])
                    P.step("scalar", lambda e, nrm=nrm: e.activation(
                        out=nrm.ap, in_=nrm.ap, func=AF.Ln),
                        reads=[nrm], writes=[nrm])
                    P.step("scalar", lambda e, nrm=nrm: e.activation(
                        out=nrm.ap, in_=nrm.ap, func=AF.Exp, scale=-0.5),
                        reads=[nrm], writes=[nrm])
                    mm_acc(p_bc, [(ind2.ap, nrm.ap, [ind2, nrm], None)])
                    P.step("scalar", lambda e, bc_sb=bc_sb, p_bc=p_bc: e.copy(
                        bc_sb.ap, p_bc.ap),
                        reads=[p_bc], writes=[bc_sb])
                    P.step("vector", lambda e, pq=pq, qn=qn, sv=sv: e.tensor_scalar(
                        qn.ap, pq.ap, sv.ap, None, op0=OP.mult),
                        reads=[pq, sv], writes=[qn])
                    mm_acc(p_rot, [(prot.ap, qn.ap, [prot, qn], None)])
                    tc_ap = tc.ap
                    ts_ap = tsn.ap
                    P.step("vector", lambda e, tc_ap=tc_ap, u=u, qn=qn: e.tensor_tensor(
                        u.ap, qn.ap, tc_ap, op=OP.mult),
                        reads=[qn, tc], writes=[u])
                    P.step("vector", lambda e, ts_ap=ts_ap, qn=qn, p_rot=p_rot:
                           e.tensor_tensor(
                        qn.ap, p_rot.ap, ts_ap, op=OP.mult),
                        reads=[p_rot, tsn], writes=[qn])
                    P.step("vector", lambda e, u=u, qn=qn: e.tensor_tensor(
                        u.ap, u.ap, qn.ap, op=OP.add),
                        reads=[u, qn], writes=[u])
                    P.step("vector", lambda e, tgt_ap=tgt_ap, u=u, bc_sb=bc_sb:
                           e.tensor_tensor(
                        tgt_ap, u.ap, bc_sb.ap, op=OP.mult),
                        reads=[u, bc_sb], writes=[tgt_buf])
                else:
                    pr = oc - 8
                    p_vt = pool[2 + 3 * (oc % 2)]
                    vfm = aux2[oc % 2]["sq"]
                    P.step("vector", lambda e, pq=pq, vfm=vfm: e.tensor_copy(
                        vfm.ap, pq.ap),
                        reads=[pq], writes=[vfm])
                    fd = ([p_vt.w] if p_vt.w else []) + p_vt.rs
                    for tt in range(4):
                        P.step("tensor", lambda e, tt=tt, p_vt=p_vt, vfm=vfm: e.transpose(
                            p_vt.ap[:, 128 * tt:128 * (tt + 1)],
                            vfm.ap[:, 128 * tt:128 * (tt + 1)], ident.ap),
                            reads=[vfm, ident], writes=[],
                            extra_deps=fd if tt == 0 else [])
                    p_vt.w = ("tensor", P.cnt["tensor"])
                    p_vt.rs = []
                    dst = vt.ap.rearrange("p (sl h c) -> p sl h c", h=8, c=65)[
                        :, sl * 4:sl * 4 + 4, 2 * pr:2 * pr + 2, 0:64]
                    src = p_vt.ap.rearrange("p (tt h c) -> p tt h c", h=2, c=64)
                    P.step("vector", lambda e, dst=dst, src=src: e.tensor_copy(dst, src),
                           reads=[p_vt], writes=[vt])

        def attn_window(w):
            sl = w % 2
            psl = (w - 1) % 2
            for h in range(L_HEADS):
                pr, hh = h // 2, h % 2
                base = 64 * hh
                kts = list(range(4, 8)) if w == 0 else list(range(8))
                for kt in kts:
                    qlo = max(0, 128 * kt - 512)
                    qhi = min(512, 128 * kt + 128)
                    slot = psl if kt < 4 else sl
                    ksub = kt % 4
                    lhs = kf.ap[base:base + 64,
                                (pr * 2 + slot) * 512 + 128 * ksub:
                                (pr * 2 + slot) * 512 + 128 * (ksub + 1)]
                    rhs = qf.ap[base:base + 64, 512 * pr + qlo: 512 * pr + qhi]
                    pc = pool[kt % 2]
                    mm_acc(pc, [(lhs, rhs, [kf, qf], pc.ap[:, 0:qhi - qlo])])
                    if kt >= 4:
                        mask, mlo = tril, 0
                    else:
                        mask, mlo = triu, (qhi - qlo) - 128
                    P.step("vector", lambda e, pc=pc, mask=mask, mlo=mlo:
                           e.tensor_tensor(pc.ap[:, mlo:mlo + 128],
                                           pc.ap[:, mlo:mlo + 128], mask.ap, op=OP.add),
                           reads=[pc, mask], writes=[pc])
                    P.step("scalar", lambda e, pc=pc, kt=kt, qlo=qlo, qhi=qhi:
                           e.activation(out=probsT.ap[:, 512 * kt + qlo: 512 * kt + qhi],
                                        in_=pc.ap[:, 0:qhi - qlo], func=AF.Exp,
                                        scale=QK_SCALE),
                           reads=[pc], writes=[probsT])
                korder = [4] + [kt for kt in kts if kt != 4]
                p_y = pool[2]
                items_y = []
                for i, kt in enumerate(korder):
                    qlo = max(0, 128 * kt - 512)
                    qhi = min(512, 128 * kt + 128)
                    slot = psl if kt < 4 else sl
                    ksub = kt % 4
                    vl = vt.ap[:, (slot * 4 + ksub) * 520 + 65 * h:
                               (slot * 4 + ksub) * 520 + 65 * (h + 1)]
                    pT = probsT.ap[:, 512 * kt + qlo: 512 * kt + qhi]
                    items_y.append((vl, pT, [vt, probsT], p_y.ap[0:65, qlo:qhi]))
                mm_acc(p_y, items_y)
                p_yb = pool[4]
                P.step("scalar", lambda e, p_y=p_y: e.activation(
                    out=rec.ap, in_=p_y.ap[64:65, :], func=AF.Ln),
                    reads=[p_y, rec], writes=[rec])
                P.step("scalar", lambda e: e.activation(
                    out=rec.ap, in_=rec.ap, func=AF.Exp, scale=-1.0),
                    reads=[rec], writes=[rec])
                mm_acc(p_yb, [(onesm.ap[64:65, 0:64], rec.ap, [onesm, rec],
                               p_yb.ap[0:64, :])])
                P.step("vector", lambda e, p_yb=p_yb: e.tensor_copy(ybc.ap,
                                                                    p_yb.ap[0:64, :]),
                       reads=[p_yb], writes=[ybc])
                P.step("vector", lambda e, p_y=p_y: e.tensor_tensor(
                    y_w.ap, p_y.ap[0:64, :], ybc.ap, op=OP.mult),
                    reads=[p_y, ybc], writes=[y_w])
                P.step("scalar", lambda e, h=h: e.copy(
                    ywh.ap[:, 512 * h: 512 * (h + 1)], y_w.ap),
                    reads=[y_w, ywh], writes=[ywh])
                P.step("vector", lambda e, h=h: e.tensor_tensor(
                    ywl.ap[:, 512 * h: 512 * (h + 1)], y_w.ap,
                    ywh.ap[:, 512 * h: 512 * (h + 1)], op=OP.subtract),
                    reads=[y_w, ywh, ywl], writes=[ywl])

        def wo_window(w):
            for h in range(L_HEADS):
                wbh, wbl = wob[h % 2]
                P.step("sync", lambda e, h=h, wbh=wbh: e.dma_start(
                    out=wbh.ap, in_=wofr[0][64 * h: 64 * (h + 1), :]),
                    reads=[], writes=[wbh], inc=16, dsem=wosem[h % 2])
                P.step("sync", lambda e, h=h, wbl=wbl: e.dma_start(
                    out=wbl.ap, in_=wofr[1][64 * h: 64 * (h + 1), :]),
                    reads=[], writes=[wbl], inc=16, dsem=wosem[2 + h % 2])
                for dc in range(NCH):
                    pw = pool[dc]
                    fd = Prog.merge(([pw.w] if (h == 0 and pw.w) else []) +
                                    (pw.rs if h == 0 else []) +
                                    [wbh.w, wbl.w, ywh.w, ywl.w])
                    triples = ((wbh, ywh), (wbh, ywl), (wbl, ywh))
                    for ti, (wb_, yw_) in enumerate(triples):
                        st = (h == 0 and ti == 0)
                        sp = (h == L_HEADS - 1 and ti == 2)
                        P.step("tensor", lambda e, wb_=wb_, yw_=yw_, dc=dc, h=h, pw=pw,
                               st=st, sp=sp:
                               e.matmul(pw.ap, wb_.ap[:, 128 * dc:128 * (dc + 1)],
                                        yw_.ap[:, 512 * h: 512 * (h + 1)],
                                        start=st, stop=sp),
                               reads_reg=[wb_, yw_], writes=[],
                               extra_deps=fd if ti == 0 else [])
                    if h == L_HEADS - 1:
                        pw.w = ("tensor", P.cnt["tensor"])
                        pw.rs = []
            for dc in range(NCH):
                pw = pool[dc]
                dst = stream[dc].ap[:, 1 + 512 * w: 1 + 512 * (w + 1)]
                P.step("vector", lambda e, dst=dst, pw=pw: e.tensor_tensor(
                    dst, dst, pw.ap, op=OP.add),
                    reads=[pw, stream[dc]], writes=[stream[dc]])

        qkv_window(0)
        for w in range(1, NWIN):
            attn_window(w - 1)
            qkv_window(w)
            wo_window(w - 1)
        attn_window(NWIN - 1)
        wo_window(NWIN - 1)
        if env.get("tapBattn") and s == 0 and l == 0:
            env["tapBattn"](nc, P, env, locals())

    P.barrier()
    with ExitStack() as LM:
        ts = lambda n, p, f, dt=F32: LM.enter_context(
            nc.sbuf_tensor(f"M{s}{l}_{n}", [p, f], dt))
        pool = [Buf(LM.enter_context(
            nc.psum_tensor(f"M{s}{l}_pp{i}", [128, 512], F32)).ap()) for i in range(8)]
        tsb = lambda n, p, f: LM.enter_context(
            nc.sbuf_tensor(f"M{s}{l}_{n}", [p, f], BF16))
        lnout = [(Buf(tsb(f"lnoh{i}", 128, NCH * 512).ap()),
                  Buf(tsb(f"lnol{i}", 128, NCH * 512).ap())) for i in range(2)]
        gg = (Buf(tsb("ggh", 128, 22 * 512).ap()), Buf(tsb("ggl", 128, 22 * 512).ap()))
        w1b = [(Buf(tsb(f"w1bh{i}", 128, 8 * 512).ap()),
                Buf(tsb(f"w1bl{i}", 128, 8 * 512).ap())) for i in range(2)]
        w1sem = [P.dma_sem(f"Mw1{i}") for i in range(4)]
        w2b = [(Buf(tsb(f"w2bh{i}", 128, 1024).ap()),
                Buf(tsb(f"w2bl{i}", 128, 1024).ap())) for i in range(2)]
        w2sem = [P.dma_sem(f"Mw2{i}") for i in range(4)]
        tmp2 = Buf(ts("tmp2", 128, 512).ap())
        ggt = Buf(ts("ggt", 128, 512).ap())
        w1sr = env["w1s"][l]
        w2sr = env["w2s"][l]
        w1v = [w1sr[i].rearrange("(a p) n -> p a n", p=128) for i in range(2)]
        vecs = Buf(ts("vecs", 128, 68).ap())
        b1a = Buf(vecs.ap[:, 0:22])
        b1g = Buf(vecs.ap[:, 22:44])
        lng = Buf(vecs.ap[:, 44:52])
        lnb = Buf(vecs.ap[:, 52:60])
        b2v = Buf(vecs.ap[:, 60:68])
        mvsem = P.dma_sem("Mv")
        for b, key in [(b1a, "ffn_b1ar"), (b1g, "ffn_b1gr"), (lng, "ffn_lngr"),
                       (lnb, "ffn_lnbr"), (b2v, "ffn_b2r")]:
            P.step("sync", lambda e, b=b, key=key: e.dma_start(out=b.ap, in_=W[key][l]),
                   reads=[], writes=[b], inc=16, dsem=mvsem)
        mtok = (mvsem, P.dma_cnt[mvsem])
        for b in (b1a, b1g, lng, lnb, b2v):
            b.w = mtok
        sqs = Buf(ts("sqs", 128, 512).ap())
        tmp1 = Buf(ts("tmp1", 128, 512).ap())
        gel = Buf(ts("gel", 128, 512).ap())
        mean = Buf(ts("mean", 1, 512).ap())
        rstd = Buf(ts("rstd", 1, 512).ap())

        def emit_ln(tb):
            lo = lnout[tb % 2]
            p_st, p_mb, p_rb = pool[4], pool[5], pool[6]
            items = []
            for dc in range(NCH):
                shift = 1 if dc < 4 else 0
                h_ap = stream[dc].ap[:, shift + 512 * tb: shift + 512 * (tb + 1)]
                items.append((onesm.ap[:, 0:1], h_ap, [onesm, stream[dc]],
                              p_st.ap[0:1, :]))
            mm_acc(p_st, items)
            P.step("vector", lambda e, p_st=p_st: e.tensor_scalar(
                mean.ap, p_st.ap[0:1, :], 1.0 / D, None, op0=OP.mult),
                reads=[p_st], writes=[mean])
            items = []
            for dc in range(NCH):
                shift = 1 if dc < 4 else 0
                h_ap = stream[dc].ap[:, shift + 512 * tb: shift + 512 * (tb + 1)]
                P.step("scalar", lambda e, h_ap=h_ap, dc=dc: e.activation(
                    out=sqs.ap, in_=h_ap, func=AF.Square),
                    reads=[stream[dc]] + ([sqs] if dc == 0 else []), writes=[sqs])
                mm_acc(p_st, [(onesm.ap[:, 0:1], sqs.ap, [onesm, sqs],
                               p_st.ap[0:1, :])]) if dc == NCH - 1 else None
            return lo

        def emit_ln2(tb):
            loh, lol = lnout[tb % 2]
            p_st, p_mb, p_rb = pool[4], pool[5], pool[6]
            items = []
            for dc in range(NCH):
                shift = 1 if dc < 4 else 0
                h_ap = stream[dc].ap[:, shift + 512 * tb: shift + 512 * (tb + 1)]
                items.append((onesm.ap[:, 0:1], h_ap, [onesm, stream[dc]],
                              p_st.ap[0:1, :]))
            mm_acc(p_st, items)
            P.step("vector", lambda e, p_st=p_st: e.tensor_scalar(
                mean.ap, p_st.ap[0:1, :], 1.0 / D, None, op0=OP.mult),
                reads=[p_st], writes=[mean])
            fd = ([p_st.w] if p_st.w else []) + p_st.rs
            for dc in range(NCH):
                shift = 1 if dc < 4 else 0
                h_ap = stream[dc].ap[:, shift + 512 * tb: shift + 512 * (tb + 1)]
                P.step("scalar", lambda e, h_ap=h_ap: e.activation(
                    out=sqs.ap, in_=h_ap, func=AF.Square),
                    reads=[stream[dc], sqs], writes=[sqs])
                P.step("tensor", lambda e, dc=dc: e.matmul(
                    p_st.ap[0:1, :], onesm.ap[:, 0:1], sqs.ap,
                    start=(dc == 0), stop=(dc == NCH - 1)),
                    reads=[onesm, sqs], writes=[],
                    extra_deps=fd if dc == 0 else [])
            p_st.w = ("tensor", P.cnt["tensor"])
            p_st.rs = []
            P.step("scalar", lambda e: e.activation(out=rstd.ap, in_=mean.ap,
                                                    func=AF.Square),
                   reads=[mean], writes=[rstd])
            P.step("vector", lambda e, p_st=p_st: e.scalar_tensor_tensor(
                rstd.ap, p_st.ap[0:1, :], 1.0 / D, rstd.ap, OP.mult, OP.subtract),
                reads=[p_st, rstd], writes=[rstd])
            P.step("vector", lambda e: e.tensor_scalar(rstd.ap, rstd.ap, 1e-5, None,
                                                       op0=OP.add),
                   reads=[rstd], writes=[rstd])
            P.step("scalar", lambda e: e.activation(out=rstd.ap, in_=rstd.ap,
                                                    func=AF.Sqrt),
                   reads=[rstd], writes=[rstd])
            P.step("vector", lambda e: e.reciprocal(rstd.ap, rstd.ap),
                   reads=[rstd], writes=[rstd])
            mm_acc(p_mb, [(onesm.ap[0:1, :], mean.ap, [onesm, mean], None)])
            mm_acc(p_rb, [(onesm.ap[0:1, :], rstd.ap, [onesm, rstd], None)])
            for dc in range(NCH):
                shift = 1 if dc < 4 else 0
                h_ap = stream[dc].ap[:, shift + 512 * tb: shift + 512 * (tb + 1)]
                P.step("vector", lambda e, h_ap=h_ap, p_mb=p_mb: e.tensor_tensor(
                    tmp1.ap, h_ap, p_mb.ap, op=OP.subtract),
                    reads=[stream[dc], p_mb, tmp1], writes=[tmp1])
                P.step("vector", lambda e, p_rb=p_rb: e.tensor_tensor(
                    tmp1.ap, tmp1.ap, p_rb.ap, op=OP.mult),
                    reads=[tmp1, p_rb], writes=[tmp1])
                P.step("vector", lambda e, dc=dc: e.tensor_scalar(
                    tmp2.ap, tmp1.ap, lng.ap[:, dc:dc + 1], lnb.ap[:, dc:dc + 1],
                    op0=OP.mult, op1=OP.add),
                    reads=[tmp1, lng, lnb, tmp2], writes=[tmp2])
                P.step("vector", lambda e, loh=loh, dc=dc: e.tensor_copy(
                    loh.ap[:, 512 * dc: 512 * (dc + 1)], tmp2.ap),
                    reads=[tmp2], writes=[loh])
                P.step("vector", lambda e, loh=loh, lol=lol, dc=dc: e.tensor_tensor(
                    lol.ap[:, 512 * dc: 512 * (dc + 1)], tmp2.ap,
                    loh.ap[:, 512 * dc: 512 * (dc + 1)], op=OP.subtract),
                    reads=[tmp2, loh], writes=[lol])

        def emit_mlp_tb(tb):
            loh, lol = lnout[tb % 2]
            spans = [(sp * 512, min(512, HID - sp * 512))
                     for sp in range(ceil_div(HID, 512))]
            for sp, (c0, cn) in enumerate(spans):
                (abh, abl), (gbh, gbl) = w1b[0], w1b[1]
                P.step("sync", lambda e, c0=c0, cn=cn, abh=abh: e.dma_start(
                    out=abh.ap[:, 0:8 * cn].rearrange("p (a n) -> p a n", n=cn),
                    in_=w1v[0][:, :, c0:c0 + cn]),
                    reads=[], writes=[abh], inc=16, dsem=w1sem[0])
                P.step("sync", lambda e, c0=c0, cn=cn, abl=abl: e.dma_start(
                    out=abl.ap[:, 0:8 * cn].rearrange("p (a n) -> p a n", n=cn),
                    in_=w1v[1][:, :, c0:c0 + cn]),
                    reads=[], writes=[abl], inc=16, dsem=w1sem[1])
                njs = ceil_div(cn, 128)
                for jj in range(njs):
                    m = min(128, cn - 128 * jj)
                    pa = pool[jj]
                    items = []
                    for dc in range(NCH):
                        wh = abh.ap[:, dc * cn + 128 * jj: dc * cn + 128 * jj + m]
                        wl = abl.ap[:, dc * cn + 128 * jj: dc * cn + 128 * jj + m]
                        xh = loh.ap[:, 512 * dc: 512 * (dc + 1)]
                        xl = lol.ap[:, 512 * dc: 512 * (dc + 1)]
                        items.append((wh, xh, [abh, loh], pa.ap[0:m, :]))
                        items.append((wh, xl, [abh, lol], pa.ap[0:m, :]))
                        items.append((wl, xh, [abl, loh], pa.ap[0:m, :]))
                    mm_acc(pa, items)
                P.step("sync", lambda e, c0=c0, cn=cn, gbh=gbh: e.dma_start(
                    out=gbh.ap[:, 0:8 * cn].rearrange("p (a n) -> p a n", n=cn),
                    in_=w1v[0][:, :, HID + c0:HID + c0 + cn]),
                    reads=[], writes=[gbh], inc=16, dsem=w1sem[2])
                P.step("sync", lambda e, c0=c0, cn=cn, gbl=gbl: e.dma_start(
                    out=gbl.ap[:, 0:8 * cn].rearrange("p (a n) -> p a n", n=cn),
                    in_=w1v[1][:, :, HID + c0:HID + c0 + cn]),
                    reads=[], writes=[gbl], inc=16, dsem=w1sem[3])
                for jj in range(njs):
                    j = (c0 + 128 * jj) // 128
                    m = min(128, cn - 128 * jj)
                    pg = pool[4 + jj]
                    items = []
                    for dc in range(NCH):
                        wh = gbh.ap[:, dc * cn + 128 * jj: dc * cn + 128 * jj + m]
                        wl = gbl.ap[:, dc * cn + 128 * jj: dc * cn + 128 * jj + m]
                        xh = loh.ap[:, 512 * dc: 512 * (dc + 1)]
                        xl = lol.ap[:, 512 * dc: 512 * (dc + 1)]
                        items.append((wh, xh, [gbh, loh], pg.ap[0:m, :]))
                        items.append((wh, xl, [gbh, lol], pg.ap[0:m, :]))
                        items.append((wl, xh, [gbl, loh], pg.ap[0:m, :]))
                    mm_acc(pg, items)
                    P.step("scalar", lambda e, pg=pg, j=j, m=m: e.activation(
                        out=gel.ap[0:m, :], in_=pg.ap[0:m, :], func=AF.Gelu,
                        bias=b1g.ap[0:m, j:j + 1]),
                        reads=[pg, b1g, gel], writes=[gel])
                    pa = pool[jj]
                    P.step("vector", lambda e, pa=pa, j=j, m=m: e.scalar_tensor_tensor(
                        ggt.ap[0:m, :], pa.ap[0:m, :],
                        b1a.ap[0:m, j:j + 1], gel.ap[0:m, :], OP.add, OP.mult),
                        reads=[pa, b1a, gel, ggt], writes=[ggt])
                    P.step("vector", lambda e, j=j, m=m: e.tensor_copy(
                        gg[0].ap[0:m, 512 * j: 512 * (j + 1)], ggt.ap[0:m, :]),
                        reads=[ggt], writes=[gg[0]])
                    P.step("vector", lambda e, j=j, m=m: e.tensor_tensor(
                        gg[1].ap[0:m, 512 * j: 512 * (j + 1)], ggt.ap[0:m, :],
                        gg[0].ap[0:m, 512 * j: 512 * (j + 1)], op=OP.subtract),
                        reads=[ggt, gg[0]], writes=[gg[1]])
            njtot = ceil_div(HID, 128)
            fd = {dc: ([pool[dc].w] if pool[dc].w else []) + pool[dc].rs
                  for dc in range(8)}
            for j in range(njtot):
                m = min(128, HID - 128 * j)
                wbh, wbl = w2b[j % 2]
                P.step("sync", lambda e, j=j, m=m, wbh=wbh: e.dma_start(
                    out=wbh.ap[0:m, :], in_=w2sr[0][128 * j: 128 * j + m, :]),
                    reads=[], writes=[wbh], inc=16, dsem=w2sem[j % 2])
                P.step("sync", lambda e, j=j, m=m, wbl=wbl: e.dma_start(
                    out=wbl.ap[0:m, :], in_=w2sr[1][128 * j: 128 * j + m, :]),
                    reads=[], writes=[wbl], inc=16, dsem=w2sem[2 + j % 2])
                for dc in range(NCH):
                    triples = ((wbh, gg[0]), (wbh, gg[1]), (wbl, gg[0]))
                    for ti, (wb_, ggi) in enumerate(triples):
                        st = (j == 0 and ti == 0)
                        sp = (j == njtot - 1 and ti == 2)
                        P.step("tensor", lambda e, wb_=wb_, ggi=ggi, dc=dc, j=j, m=m,
                               st=st, sp=sp:
                               e.matmul(pool[dc].ap,
                                        wb_.ap[0:m, 128 * dc: 128 * (dc + 1)],
                                        ggi.ap[0:m, 512 * j: 512 * (j + 1)],
                                        start=st, stop=sp),
                               reads=[wb_, ggi], writes=[],
                               extra_deps=fd[dc] if st else [])
            for dc in range(NCH):
                pool[dc].w = ("tensor", P.cnt["tensor"])
                pool[dc].rs = []
                dst = stream[dc].ap[:, 1 + 512 * tb: 1 + 512 * (tb + 1)]
                P.step("vector", lambda e, dst=dst, dc=dc: e.scalar_tensor_tensor(
                    dst, pool[dc].ap, b2v.ap[:, dc:dc + 1], dst, OP.add, OP.add),
                    reads=[pool[dc], b2v, stream[dc]], writes=[stream[dc]])

        emit_ln2(0)
        for tb in range(4):
            if tb + 1 < 4:
                emit_ln2(tb + 1)
            emit_mlp_tb(tb)


def split_weights(nc, P, G, env, W, w1s, w2s, wqkvs, wofs):
    with ExitStack() as L:
        f32 = Buf(L.enter_context(nc.sbuf_tensor("sw_f32", [128, 5460], F32)).ap())
        hi = Buf(L.enter_context(nc.sbuf_tensor("sw_hi", [128, 5460], BF16)).ap())
        lo = Buf(L.enter_context(nc.sbuf_tensor("sw_lo", [128, 5460], BF16)).ap())
        insem = P.dma_sem("swin")
        outsem_h = P.dma_sem("swouth")
        outsem_l = P.dma_sem("swoutl")
        jobs = []
        for l in range(2):
            for dc in range(NCH):
                jobs.append((W["ffn_w1"][l][128 * dc:128 * (dc + 1), :],
                             w1s[l][0][128 * dc:128 * (dc + 1), :],
                             w1s[l][1][128 * dc:128 * (dc + 1), :], 5460, 128))
            for dc in range(NCH):
                jobs.append((W["ffn_wqkv"][l][128 * dc:128 * (dc + 1), :],
                             wqkvs[l][0][128 * dc:128 * (dc + 1), :],
                             wqkvs[l][1][128 * dc:128 * (dc + 1), :], 1536, 128))
            for j in range(ceil_div(HID, 128)):
                m = min(128, HID - 128 * j)
                jobs.append((W["ffn_w2"][l][128 * j:128 * j + m, :],
                             w2s[l][0][128 * j:128 * j + m, :],
                             w2s[l][1][128 * j:128 * j + m, :], D, m))
            for j in range(4):
                jobs.append((W["ffn_wo"][l][128 * j:128 * (j + 1), :],
                             wofs[l][0][128 * j:128 * (j + 1), :],
                             wofs[l][1][128 * j:128 * (j + 1), :], D, 128))
        for src_ap, hi_ap, lo_ap, n, m in jobs:
            P.step("sync", lambda e, src_ap=src_ap, n=n, m=m: e.dma_start(
                out=f32.ap[0:m, 0:n], in_=src_ap),
                reads=[], writes=[f32], inc=16, dsem=insem)
            P.step("vector", lambda e, n=n, m=m: e.tensor_copy(hi.ap[0:m, 0:n],
                                                               f32.ap[0:m, 0:n]),
                   reads=[f32], writes=[hi])
            P.step("vector", lambda e, n=n, m=m: e.tensor_tensor(
                lo.ap[0:m, 0:n], f32.ap[0:m, 0:n], hi.ap[0:m, 0:n], op=OP.subtract),
                reads=[f32, hi], writes=[lo])
            P.step("gpsimd", lambda e, hi_ap=hi_ap, n=n, m=m: e.dma_start(
                out=hi_ap, in_=hi.ap[0:m, 0:n]),
                reads=[hi], writes=[], inc=16, dsem=outsem_h)
            P.step("gpsimd", lambda e, lo_ap=lo_ap, n=n, m=m: e.dma_start(
                out=lo_ap, in_=lo.ap[0:m, 0:n]),
                reads=[lo], writes=[], inc=16, dsem=outsem_l)
    P.barrier()


def precompute_AB(nc, P, G, env, W, park_ab):
    ident = env["ident"]
    with ExitStack() as L:
        ts = lambda n, p, f: L.enter_context(nc.sbuf_tensor(f"AB_{n}", [p, f], F32))
        pool = [Buf(L.enter_context(
            nc.psum_tensor(f"AB_pp{i}", [128, 512], F32)).ap()) for i in range(4)]
        kup = Buf(ts("kup", 32, 128).ap())
        vup = Buf(ts("vup", 32, 128).ap())
        kupT = Buf(ts("kupT", 128, 32).ap())
        vupT = Buf(ts("vupT", 128, 32).ap())
        wqs = Buf(ts("wqs", 128, 8 * 128).ap())
        wqT = Buf(ts("wqT", 128, 8 * 128).ap())
        wos = Buf(ts("wos", 128, 1024).ap())
        At = Buf(ts("At", 32, 8 * 128).ap())
        Bt = Buf(ts("Bt", 32, 1024).ap())
        A_all = Buf(ts("A_all", 128, 2 * 8 * 128).ap())
        B_all = Buf(ts("B_all", 128, 2 * 1024).ap())
        absem = P.dma_sem("ABload")
        P.step("sync", lambda e: e.dma_start(out=kup.ap, in_=W["attn_wk_up"][:, :]),
               reads=[], writes=[kup], inc=16, dsem=absem)
        P.step("sync", lambda e: e.dma_start(out=vup.ap, in_=W["attn_wv_up"][:, :]),
               reads=[], writes=[vup], inc=16, dsem=absem)
        tok = (absem, P.dma_cnt[absem])
        kup.w = tok
        vup.w = tok
        for src, dst in ((kup, kupT), (vup, vupT)):
            P.step("tensor", lambda e, src=src: e.transpose(
                pool[0].ap[:, 0:32], src.ap, ident.ap[0:32, 0:32]),
                reads=[src, ident], writes=[pool[0]])
            P.step("vector", lambda e, dst=dst: e.tensor_copy(dst.ap, pool[0].ap[:, 0:32]),
                   reads=[pool[0]], writes=[dst])
        wqr = W["attn_wq"].rearrange("(a p) n -> p a n", p=128)
        wsem = P.dma_sem("ABwq")
        osem = P.dma_sem("ABwo")
        for h in range(MLA_HEADS):
            g, r = h // 4, h % 4
            P.step("sync", lambda e, h=h: e.dma_start(
                out=wqs.ap.rearrange("p (a n) -> p a n", n=128),
                in_=wqr[:, :, 128 * h:128 * (h + 1)]),
                reads=[], writes=[wqs], inc=16, dsem=wsem)
            for dc in range(NCH):
                pb = pool[dc % 2]
                P.step("tensor", lambda e, dc=dc, pb=pb: e.transpose(
                    pb.ap[:, 0:128], wqs.ap[:, 128 * dc:128 * (dc + 1)], ident.ap),
                    reads=[wqs, ident], writes=[pb])
                P.step("vector", lambda e, dc=dc, pb=pb: e.tensor_copy(
                    wqT.ap[:, 128 * dc:128 * (dc + 1)], pb.ap[:, 0:128]),
                    reads=[pb], writes=[wqT])
            for dc in range(NCH):
                pb = pool[2 + dc % 2]
                P.step("tensor", lambda e, dc=dc, pb=pb: e.matmul(
                    pb.ap[0:32, 0:128], kupT.ap, wqT.ap[:, 128 * dc:128 * (dc + 1)],
                    start=True, stop=True),
                    reads=[kupT, wqT], writes=[pb])
                P.step("vector", lambda e, dc=dc, pb=pb: e.tensor_scalar(
                    At.ap[:, 128 * dc:128 * (dc + 1)], pb.ap[0:32, 0:128],
                    MLA_SCALE, None, op0=OP.mult),
                    reads=[pb], writes=[At])
            for dc in range(NCH):
                pb = pool[dc % 2]
                P.step("tensor", lambda e, dc=dc, pb=pb: e.transpose(
                    pb.ap[:, 0:32], At.ap[:, 128 * dc:128 * (dc + 1)],
                    ident.ap[0:32, 0:32]),
                    reads=[At, ident], writes=[pb])
                col = (g * 8 + dc) * 128 + r * 32
                P.step("vector", lambda e, col=col, pb=pb: e.tensor_copy(
                    A_all.ap[:, col:col + 32], pb.ap[:, 0:32]),
                    reads=[pb], writes=[A_all])
            P.step("sync", lambda e, h=h: e.dma_start(
                out=wos.ap, in_=W["attn_wo"][128 * h:128 * (h + 1), :]),
                reads=[], writes=[wos], inc=16, dsem=osem)
            pb = pool[2]
            for half in range(2):
                mm_acc2(P, pb, [(vupT.ap, wos.ap[:, 512 * half: 512 * (half + 1)],
                                 [vupT, wos], pb.ap[0:32, :])])
                P.step("vector", lambda e, half=half, pb=pb: e.tensor_copy(
                    Bt.ap[:, 512 * half:512 * (half + 1)], pb.ap[0:32, :]),
                    reads=[pb], writes=[Bt])
            P.step("gpsimd", lambda e, g=g, r=r: e.dma_start(
                out=B_all.ap[32 * r:32 * (r + 1), 1024 * g:1024 * (g + 1)],
                in_=Bt.ap),
                reads=[Bt], writes=[B_all], inc=16, dsem=P.dma_sem("ABmv"))
        psem = P.dma_sem("ABpark")
        P.step("gpsimd", lambda e: e.dma_start(out=park_ab[0], in_=A_all.ap),
               reads=[A_all], writes=[], inc=16, dsem=psem)
        P.step("gpsimd", lambda e: e.dma_start(out=park_ab[1], in_=B_all.ap),
               reads=[B_all], writes=[], inc=16, dsem=psem)
    P.barrier()


def phase_C(nc, P, G, env, W, park_xn, park_ab, s):
    stream = env["stream"]
    onesm, triu, ident = env["onesm"], env["triu"], env["ident"]

    with ExitStack() as LC:
        ts = lambda n, p, f: LC.enter_context(nc.sbuf_tensor(f"C{s}_{n}", [p, f], F32))
        klat = [Buf(ts(f"klat{g}", 128, T).ap()) for g in range(2)]
        vlat_tm = Buf(ts("vlat_tm", 128, 16 * 64).ap())
        qt = [Buf(ts(f"qt{g}", 128, T).ap()) for g in range(2)]
        B_all = Buf(ts("B_all", 128, 2 * 1024).ap())
        absem2 = P.dma_sem("Cab")
        P.step("sync", lambda e: e.dma_start(out=B_all.ap, in_=park_ab[1]),
               reads=[], writes=[B_all], inc=16, dsem=absem2,
               extra_deps=[("ABpark", 32)])

        with ExitStack() as LS:
            ts2 = lambda n, p, f: LS.enter_context(
                nc.sbuf_tensor(f"C{s}s_{n}", [p, f], F32))
            pool = [Buf(LS.enter_context(
                nc.psum_tensor(f"C{s}_pp{i}", [128, 512], F32)).ap()) for i in range(8)]
            xn_t = ts2("xn", 128, NCH * CW)
            xnc = [Buf(xn_t[:, c * CW + 1: c * CW + 1 + T]) for c in range(NCH)]
            A_all = Buf(ts2("A_all", 128, 2 * 8 * 128).ap())
            kdr = Buf(ts2("kdr", 128, 2 * 8 * 128).ap())
            vdn = Buf(ts2("vdn", 128, 8 * 64).ap())
            vlat_f = Buf(ts2("vlat_f", 64, T).ap())
            xsem = P.dma_sem("Cxn")
            ptok = (env["park_sem"], P.dma_cnt[env["park_sem"]])
            xtok = P.step("sync", lambda e: e.dma_start(out=xn_t[:, :], in_=park_xn[s]),
                          reads=[], writes=[], inc=16, dsem=xsem, extra_deps=[ptok])
            for b in xnc:
                b.w = xtok
            P.step("sync", lambda e: e.dma_start(out=A_all.ap, in_=park_ab[0]),
                   reads=[], writes=[A_all], inc=16, dsem=absem2,
                   extra_deps=[("ABpark", 32)])
            wdsem = P.dma_sem("Cwd")
            P.step("sync", lambda e: e.dma_start(out=kdr.ap, in_=W["attn_wkdr"][:, :]),
                   reads=[], writes=[kdr], inc=16, dsem=wdsem)
            P.step("sync", lambda e: e.dma_start(out=vdn.ap, in_=W["attn_wvdr"][:, :]),
                   reads=[], writes=[vdn], inc=16, dsem=wdsem)
            wdtok = (wdsem, P.dma_cnt[wdsem])
            kdr.w = wdtok
            vdn.w = wdtok
            for g in range(2):
                for tb in range(4):
                    pb = pool[tb % 2]
                    items = [(kdr.ap[:, (g * 8 + dc) * 128:(g * 8 + dc + 1) * 128],
                              xnc[dc].ap[:, 512 * tb:512 * (tb + 1)],
                              [kdr, xnc[dc]], None) for dc in range(NCH)]
                    mm_acc2(P, pb, items)
                    P.step("scalar", lambda e, g=g, tb=tb, pb=pb: e.copy(
                        klat[g].ap[:, 512 * tb:512 * (tb + 1)], pb.ap),
                        reads=[pb], writes=[klat[g]])
            for tb in range(4):
                pb = pool[2 + tb % 2]
                items = [(vdn.ap[:, dc * 64:(dc + 1) * 64],
                          xnc[dc].ap[:, 512 * tb:512 * (tb + 1)],
                          [vdn, xnc[dc]], pb.ap[0:64, :]) for dc in range(NCH)]
                mm_acc2(P, pb, items)
                P.step("scalar", lambda e, tb=tb, pb=pb: e.copy(
                    vlat_f.ap[:, 512 * tb:512 * (tb + 1)], pb.ap[0:64, :]),
                    reads=[pb], writes=[vlat_f])
            for kt in range(16):
                pb = pool[4 + kt % 2]
                P.step("tensor", lambda e, kt=kt, pb=pb: e.transpose(
                    pb.ap[:, 0:64], vlat_f.ap[:, 128 * kt:128 * (kt + 1)],
                    ident.ap[0:64, 0:64]),
                    reads=[vlat_f, ident], writes=[pb])
                P.step("vector", lambda e, kt=kt, pb=pb: e.tensor_copy(
                    vlat_tm.ap[:, 64 * kt:64 * (kt + 1)], pb.ap[:, 0:64]),
                    reads=[pb], writes=[vlat_tm])
            for g in range(2):
                for tb in range(4):
                    pb = pool[6 + tb % 2]
                    items = [(A_all.ap[:, (g * 8 + dc) * 128:(g * 8 + dc + 1) * 128],
                              xnc[dc].ap[:, 512 * tb:512 * (tb + 1)],
                              [A_all, xnc[dc]], None) for dc in range(NCH)]
                    mm_acc2(P, pb, items)
                    P.step("scalar", lambda e, g=g, tb=tb, pb=pb: e.copy(
                        qt[g].ap[:, 512 * tb:512 * (tb + 1)], pb.ap),
                        reads=[pb], writes=[qt[g]])
        P.barrier()
        psc = [Buf(LC.enter_context(
            nc.psum_tensor(f"C{s}_sc{p}", [128, 2048], F32)).ap()) for p in range(2)]
        probs = [Buf(ts(f"probs{p}", 128, T).ap()) for p in range(2)]
        probsT = [Buf(ts(f"probsT{p}", 128, T).ap()) for p in range(2)]
        ztile = [Buf(ts(f"ztile{p}", 128, 128).ap()) for p in range(2)]
        z_fm = [Buf(ts(f"zfm{g}", 128, T).ap()) for g in range(2)]
        nmax = [Buf(ts(f"nmax{p}", 128, 1).ap()) for p in range(2)]
        rsum = [Buf(ts(f"rsum{p}", 128, 1).ap()) for p in range(2)]
        rsum2 = [Buf(ts(f"rsum2{p}", 128, 1).ap()) for p in range(2)]

        rcnt = 0
        for g in range(2):
            for qi in range(16):
                for r in range(4):
                    h = g * 4 + r
                    par = rcnt % 2
                    rcnt += 1
                    PS = psc[par]
                    prb, prT = probs[par], probsT[par]
                    nmx, rsm = nmax[par], rsum[par]
                    zt = ztile[qi % 2]
                    nk = qi + 1
                    nblk = ceil_div(nk * 128, 512)
                    fd = Prog.merge(([PS.w] if PS.w else []) + PS.rs +
                                    [qt[g].w, klat[g].w])
                    first = True
                    for bkk in range(nblk):
                        n0 = 512 * bkk
                        n1 = min(nk * 128, n0 + 512)
                        lhs = qt[g].ap[32 * r:32 * (r + 1), 128 * qi:128 * (qi + 1)]
                        rhs = klat[g].ap[32 * r:32 * (r + 1), n0:n1]
                        P.step("tensor", lambda e, lhs=lhs, rhs=rhs, PS=PS, n0=n0,
                               n1=n1, r=r: e.matmul(PS.ap[:, n0:n1], lhs, rhs,
                                                    start=True, stop=True,
                                                    tile_position=(32 * r, 0)),
                               reads_reg=[qt[g], klat[g]], writes=[],
                               extra_deps=fd if first else [])
                        first = False
                    PS.w = ("tensor", P.cnt["tensor"])
                    PS.rs = []
                    dof = (nk - 1) * 128
                    P.step("vector", lambda e, PS=PS, dof=dof: e.tensor_tensor(
                        PS.ap[:, dof:dof + 128],
                        PS.ap[:, dof:dof + 128], triu.ap, op=OP.add),
                        reads=[PS, triu], writes=[PS])
                    P.step("vector", lambda e, PS=PS, nmx=nmx, n=nk * 128: e.reduce_max(
                        nmx.ap, PS.ap[:, 0:n], axis=AX.X, negate=True),
                        reads=[PS, nmx], writes=[nmx])
                    P.step("scalar", lambda e, PS=PS, prb=prb, nmx=nmx, rsm=rsm,
                           n=nk * 128: e.activation(
                        out=prb.ap[:, 0:n], in_=PS.ap[:, 0:n],
                        func=AF.Exp, bias=nmx.ap, accum_out=rsm.ap),
                        reads=[PS, nmx, rsm], writes=[prb, rsm])
                    P.step("vector", lambda e, rsm=rsm: e.reciprocal(rsm.ap, rsm.ap),
                           reads=[rsm], writes=[rsm])
                    ntp = 0
                    tb_i = 0
                    pb_ap = lambda i: PS.ap[:, 512 * i:512 * (i + 1)]
                    fdT = ([PS.w] if PS.w else []) + PS.rs
                    for kt in range(nk):
                        sl = ntp % 4
                        P.step("tensor", lambda e, kt=kt, sl=sl, tb_i=tb_i, PS=PS,
                               prb=prb: e.transpose(
                            PS.ap[:, 512 * tb_i + 128 * sl:512 * tb_i + 128 * (sl + 1)],
                            prb.ap[:, 128 * kt:128 * (kt + 1)], ident.ap),
                            reads=[prb, ident], writes=[],
                            extra_deps=fdT if ntp == 0 else [])
                        ntp += 1
                        if sl == 3 or kt == nk - 1:
                            PS.w = ("tensor", P.cnt["tensor"])
                            PS.rs = []
                            lo = 128 * (kt - sl)
                            P.step("scalar", lambda e, lo=lo, sl=sl, tb_i=tb_i, PS=PS,
                                   prT=prT: e.copy(
                                prT.ap[:, lo:lo + 128 * (sl + 1)],
                                PS.ap[:, 512 * tb_i:512 * tb_i + 128 * (sl + 1)]),
                                reads=[PS], writes=[prT])
                            tb_i = (tb_i + 1) % 3
                            fdT = ([PS.w] if PS.w else []) + PS.rs
                            ntp = 0
                    zreg = PS.ap[:, 1536 + 256 * (rcnt % 2):1536 + 256 * (rcnt % 2) + 32]
                    fdz = ([PS.w] if PS.w else []) + PS.rs
                    nz = len(range(nk))
                    for i, kt in enumerate(range(nk)):
                        P.step("tensor", lambda e, kt=kt, zreg=zreg, prT=prT,
                               st=(i == 0), sp=(i == nz - 1), g=g: e.matmul(
                            zreg, prT.ap[:, 128 * kt:128 * (kt + 1)],
                            vlat_tm.ap[:, 64 * kt + 32 * g: 64 * kt + 32 * (g + 1)],
                            start=st, stop=sp),
                            reads_reg=[prT, vlat_tm], writes=[],
                            extra_deps=Prog.merge(fdz + [prT.w, vlat_tm.w])
                            if i == 0 else [])
                    PS.w = ("tensor", P.cnt["tensor"])
                    PS.rs = []
                    P.step("vector", lambda e, r=r, zreg=zreg, zt=zt, rsm=rsm, PS=PS:
                           e.tensor_scalar(
                        zt.ap[:, 32 * r:32 * (r + 1)], zreg, rsm.ap,
                        None, op0=OP.mult),
                        reads=[PS, rsm], writes=[zt])
                PS = psc[(rcnt - 1) % 2]
                zt = ztile[qi % 2]
                fdz = ([PS.w] if PS.w else []) + PS.rs
                P.step("tensor", lambda e, PS=PS, zt=zt: e.transpose(
                    PS.ap[:, 1536:1664], zt.ap, ident.ap),
                    reads=[zt, ident], writes=[], extra_deps=fdz)
                PS.w = ("tensor", P.cnt["tensor"])
                PS.rs = []
                P.step("scalar", lambda e, g=g, qi=qi, PS=PS: e.copy(
                    z_fm[g].ap[:, 128 * qi:128 * (qi + 1)], PS.ap[:, 1536:1664]),
                    reads=[PS], writes=[z_fm[g]])
        for dc in range(NCH):
            for tb in range(4):
                pb = psc[tb % 2]
                items = [(B_all.ap[:, 1024 * g + 128 * dc: 1024 * g + 128 * (dc + 1)],
                          z_fm[g].ap[:, 512 * tb:512 * (tb + 1)],
                          [B_all, z_fm[g]], pb.ap[:, 0:512]) for g in range(2)]
                mm_acc2(P, pb, items)
                dst = stream[dc].ap[:, 1 + 512 * tb: 1 + 512 * (tb + 1)]
                P.step("vector", lambda e, dst=dst, pb=pb: e.tensor_tensor(
                    dst, dst, pb.ap[:, 0:512], op=OP.add),
                    reads=[pb, stream[dc]], writes=[stream[dc]])


def mm_acc2(P, pb, items):
    fd = ([pb.w] if pb.w else []) + pb.rs
    for _, _, rd, _ in items:
        fd += [b.w for b in rd]
    fd = Prog.merge(fd)
    n = len(items)
    for i, (lhs, rhs, rd, osl) in enumerate(items):
        out = pb.ap if osl is None else osl
        P.step("tensor", lambda e, lhs=lhs, rhs=rhs, out=out, st=(i == 0),
               sp=(i == n - 1): e.matmul(out, lhs, rhs, start=st, stop=sp),
               reads_reg=rd, writes=[], extra_deps=fd if i == 0 else [])
    pb.w = ("tensor", P.cnt["tensor"])
    pb.rs = []


def phase_D(nc, P, G, env, x_in, out_t, s):
    stream = env["stream"]
    ident = env["ident"]
    with ExitStack() as L:
        ts = lambda n, p, f: L.enter_context(nc.sbuf_tensor(f"D{s}_{n}", [p, f], F32))
        xt = [Buf(ts(f"xt{i}", 128, D).ap()) for i in range(2)]
        ot = [Buf(ts(f"ot{i}", 128, D).ap()) for i in range(2)]
        pt = [Buf(L.enter_context(nc.psum_tensor(f"D{s}_pt{i}", [128, D], F32)).ap())
              for i in range(2)]
        if "dx_sems" not in env:
            env["dx_sems"] = [P.dma_sem("dx0"), P.dma_sem("dx1")]
            env["do_sems"] = [P.dma_sem("do0"), P.dma_sem("do1")]
        dx = env["dx_sems"]
        do = env["do_sems"]
        for i in range(16):
            b = i % 2
            P.step("sync", lambda e, i=i, b=b: e.dma_start(
                out=xt[b].ap, in_=x_in[s, 128 * i:128 * (i + 1), :]),
                reads=[], writes=[xt[b]], inc=16, dsem=dx[b])
            fd = ([pt[b].w] if pt[b].w else []) + pt[b].rs
            for dc in range(NCH):
                P.step("tensor", lambda e, i=i, dc=dc, b=b: e.transpose(
                    pt[b].ap[:, 128 * dc:128 * (dc + 1)],
                    stream[dc].ap[:, 1 + 128 * i: 1 + 128 * (i + 1)], ident.ap),
                    reads=[stream[dc], ident], writes=[],
                    extra_deps=fd if dc == 0 else [])
            pt[b].w = ("tensor", P.cnt["tensor"])
            pt[b].rs = []
            P.step("vector", lambda e, b=b: e.tensor_tensor(
                ot[b].ap, xt[b].ap, pt[b].ap, op=OP.add),
                reads=[xt[b], pt[b]], writes=[ot[b]])
            P.step("gpsimd", lambda e, i=i, b=b: e.dma_start(
                out=out_t[s, 128 * i:128 * (i + 1), :], in_=ot[b].ap),
                reads=[ot[b]], writes=[], inc=16, dsem=do[b])


def phase_A(nc, P, G, env, x_in, park_xn, s):
    stream = env["stream"]
    stream_t = env["stream_t"]
    ident = env["ident"]
    with ExitStack() as L:
        ts = lambda n, p, f, dt=F32: L.enter_context(nc.sbuf_tensor(f"A{s}_{n}", [p, f], dt))
        xt = [Buf(ts(f"xt{i}", 128, D).ap()) for i in range(2)]
        if "xa_sems" not in env:
            env["xa_sems"] = [P.dma_sem("xa0"), P.dma_sem("xa1")]
            env["park_sem"] = P.dma_sem("park")
        xa = env["xa_sems"]
        xn = [Buf(ts(f"xn{i}", 128, D).ap()) for i in range(2)]
        st6 = [Buf(ts(f"st6{i}", 128, 12).ap()) for i in range(2)]
        smv = [Buf(ts(f"smv{i}", 128, 2).ap()) for i in range(2)]
        rst = [Buf(ts(f"rst{i}", 128, 1).ap()) for i in range(2)]
        pt = [Buf(L.enter_context(nc.psum_tensor(f"A{s}_pt{i}", [128, D], F32)).ap())
              for i in range(2)]

        eps_v = Buf(ts("eps_v", 128, 1).ap())
        P.step("vector", lambda e: e.memset(eps_v.ap, 1e-5), reads=[eps_v], writes=[eps_v])
        zcols = stream_t[:, :].rearrange("p (c w) -> p c w", w=CW)[:, :, 0:1]
        ztok = P.em("vector", lambda e: e.memset(zcols, 0.0),
                    [b.w for b in stream] + [r for b in stream for r in b.rs], 1)
        for b in stream:
            b.w = ztok

        for i in range(16):
            b = i % 2
            x_src = x_in[s, 128 * i:128 * (i + 1), :]
            P.step("sync", lambda e, x_src=x_src, b=b: e.dma_start(out=xt[b].ap, in_=x_src),
                   reads=[], writes=[xt[b]], inc=16, dsem=xa[b])
            P.step("vector", lambda e, b=b: e.bn_stats(st6[b].ap[:, 0:6],
                                                       xt[b].ap[:, 0:512]),
                   reads=[xt[b], st6[b]], writes=[st6[b]])
            P.step("vector", lambda e, b=b: e.bn_stats(st6[b].ap[:, 6:12],
                                                       xt[b].ap[:, 512:1024]),
                   reads=[xt[b], st6[b]], writes=[st6[b]])
            P.step("vector", lambda e, b=b: e.bn_aggr(smv[b].ap, st6[b].ap),
                   reads=[st6[b], smv[b]], writes=[smv[b]])
            P.step("scalar", lambda e, b=b: e.activation(out=rst[b].ap,
                                                         in_=smv[b].ap[:, 1:2],
                                                         func=AF.Sqrt,
                                                         bias=eps_v.ap),
                   reads=[smv[b], rst[b]], writes=[rst[b]])
            P.step("vector", lambda e, b=b: e.reciprocal(rst[b].ap, rst[b].ap),
                   reads=[rst[b]], writes=[rst[b]])
            P.step("vector", lambda e, b=b: e.tensor_scalar(xn[b].ap, xt[b].ap,
                                                            smv[b].ap[:, 0:1],
                                                            rst[b].ap, op0=OP.subtract,
                                                            op1=OP.mult),
                   reads=[xt[b], smv[b], rst[b]], writes=[xn[b]])
            first_deps = ([pt[b].w] if pt[b].w else []) + pt[b].rs
            for c in range(NCH):
                tok = P.step("tensor", lambda e, b=b, c=c: e.transpose(
                    pt[b].ap[:, 128 * c:128 * (c + 1)], xn[b].ap[:, 128 * c:128 * (c + 1)],
                    ident.ap),
                    reads=[xn[b], ident], writes=[],
                    extra_deps=first_deps if c == 0 else [])
            pt[b].w = tok
            pt[b].rs = []
            dst = stream_t[:, :].rearrange("p (c w) -> p c w", w=CW)[:, :, 1 + 128 * i: 1 + 128 * (i + 1)]
            src = pt[b].ap.rearrange("p (c w) -> p c w", w=128)
            P.step("vector", lambda e, dst=dst, src=src: e.tensor_copy(dst, src),
                   reads=[pt[b]] + stream, writes=stream)
        tok = P.step("gpsimd", lambda e: e.dma_start(out=park_xn[s], in_=stream_t[:, :]),
                     reads=stream, writes=[], inc=16, dsem=env["park_sem"])
        env["park_tok"] = tok



def host_constants():
    ident = np.eye(128, dtype=np.float32)
    P64 = np.zeros((64, 64), np.float32)
    for d in range(32):
        P64[d, d + 32] = -1.0
        P64[d + 32, d] = 1.0
    Pm = np.zeros((128, 128), np.float32)
    Pm[0:64, 0:64] = P64
    Pm[64:128, 64:128] = P64
    protT = Pm.T.copy()
    idx = np.arange(128)
    tri_up = np.where(idx[:, None] < idx[None, :], np.float32(NEG), np.float32(0.0)).astype(np.float32)
    tri_lo = np.where(idx[:, None] > idx[None, :], np.float32(NEG), np.float32(0.0)).astype(np.float32)
    ind2 = np.zeros((2, 128), np.float32)
    ind2[0, 0:64] = 1.0
    ind2[1, 64:128] = 1.0
    dprime = (np.arange(128) % 32).astype(np.float64)
    invf64 = 10000.0 ** (-2.0 * dprime / 64.0)
    base64 = (2.0 * dprime + 0.4 * 64.0) / (1.4 * 64.0)
    lnb64 = np.log(base64)
    def hilo(v64):
        hi = v64.astype(np.float32)
        lo = (v64 - hi.astype(np.float64)).astype(np.float32)
        return hi, lo
    ih, il = hilo(invf64)
    lh, ll = hilo(lnb64)
    tabc = np.stack([ih, il, lh, ll]).reshape(4, 128, 1).astype(np.float32)
    return {"identc": ident, "protT": protT, "tri_up": tri_up, "tri_lo": tri_lo,
            "ind2c": ind2, "tabc": tabc}


def host_relayouts(inputs):
    out = {}
    qs = np.asarray(inputs["L0_ffn_q_scale"])
    ks = np.asarray(inputs["L0_ffn_k_scale"])
    idx = np.arange(128) % 64
    out["ffn_qsr"] = np.ascontiguousarray(qs[:, idx][:, :, None])
    out["ffn_ksr"] = np.ascontiguousarray(ks[:, idx][:, :, None])
    b1 = np.asarray(inputs["L0_ffn_b1"])
    b1a = np.zeros((2, 22 * 128), np.float32)
    b1a[:, :HID] = b1[:, :HID]
    b1g = np.zeros((2, 22 * 128), np.float32)
    b1g[:, :HID] = b1[:, HID:]
    out["ffn_b1ar"] = np.ascontiguousarray(
        b1a.reshape(2, 22, 128).transpose(0, 2, 1))
    out["ffn_b1gr"] = np.ascontiguousarray(
        b1g.reshape(2, 22, 128).transpose(0, 2, 1))
    for key, name in [("L0_ffn_ln_g", "ffn_lngr"), ("L0_ffn_ln_b", "ffn_lnbr"),
                      ("L0_ffn_b2", "ffn_b2r")]:
        v = np.asarray(inputs[key])
        out[name] = np.ascontiguousarray(v.reshape(2, 8, 128).transpose(0, 2, 1))
    wkd = np.asarray(inputs["L0_attn_wk_down"])
    kdr = np.zeros((128, 2048), np.float32)
    for g in range(2):
        for dc in range(8):
            tile = np.zeros((128, 128), np.float32)
            for r in range(4):
                tile[:, 32 * r:32 * (r + 1)] = wkd[128 * dc:128 * (dc + 1),
                                                   32 * g:32 * (g + 1)]
            kdr[:, (g * 8 + dc) * 128:(g * 8 + dc + 1) * 128] = tile
    out["attn_wkdr"] = kdr
    wvd = np.asarray(inputs["L0_attn_wv_down"])
    vdr = np.zeros((128, 512), np.float32)
    for dc in range(8):
        vdr[:, 64 * dc:64 * (dc + 1)] = wvd[128 * dc:128 * (dc + 1), :]
    out["attn_wvdr"] = vdr
    return out


def kernel(**inputs):
    x = np.ascontiguousarray(inputs["L0_attn_x"])
    nc, _ = build_program()
    consts = host_constants()
    relay = host_relayouts(inputs)
    base = {}
    for k, v in inputs.items():
        if k == "L0_attn_x":
            continue
        base[k[3:]] = np.ascontiguousarray(v)
    base.update(consts)
    base.update(relay)
    in_maps = []
    for core in range(8):
        m = dict(base)
        m["x"] = x[core * NSEQ:(core + 1) * NSEQ]
        in_maps.append(m)
    res = run_bass_kernel_spmd(nc, in_maps, core_ids=list(range(8)))
    return np.concatenate([r["out"] for r in res.results], axis=0)


# version 13
# speedup vs baseline: 1.0829x; 1.0829x over previous
import math
from contextlib import ExitStack

import numpy as np

import concourse.bass as bass
import concourse.mybir as mybir
from concourse.bass_utils import run_bass_kernel_spmd

F32 = mybir.dt.float32
BF16 = mybir.dt.bfloat16
I32 = mybir.dt.int32
AF = mybir.ActivationFunctionType
OP = mybir.AluOpType
AX = mybir.AxisListType

NSEQ = 4
T = 2048
D = 1024
NCH = 8
CW = T + 1
MLA_HEADS = 8
MLA_KV = 2
MLA_DH = 128
MLA_LAT = 32
MLA_SCALE = float(MLA_DH) ** -0.5
L_HEADS = 8
L_DH = 64
WINDOW = 512
NWIN = T // WINDOW
INNER = L_HEADS * L_DH
HID = 2730
QK_SCALE = 8.0
NEG = -1.0e30

TWO_PI = 2.0 * math.pi
PI_F32 = float(np.float32(math.pi))
CW_C1 = float(np.float32(6.28125))
CW_C2 = float(np.float32(TWO_PI - CW_C1))
CW_C3 = float(np.float64(TWO_PI) - CW_C1 - np.float64(np.float32(TWO_PI - CW_C1)))

ENGS = ("sync", "tensor", "vector", "scalar", "gpsimd")


class Buf:
    __slots__ = ("ap", "w", "rs")

    def __init__(self, ap):
        self.ap = ap
        self.w = None
        self.rs = []


class Prog:

    def __init__(self, nc):
        self.nc = nc
        self.steps = {e: [] for e in ENGS}
        self.cnt = {e: 0 for e in ENGS}
        self.dma_cnt = {}
        self.dma_sems = []

    def dma_sem(self, name):
        if name not in self.dma_cnt:
            self.dma_cnt[name] = 0
            self.dma_sems.append(name)
        return name

    def em(self, eng, fn, deps=(), inc=1, dsem=None):
        deps = [d for d in deps if d is not None]
        for (pe, pv) in deps:
            lim = self.cnt[pe] if pe in self.cnt else self.dma_cnt[pe]
            assert pv <= lim, f"dep on future: {pe} {pv} > {lim}"
        self.steps[eng].append((deps, fn, inc, dsem))
        if dsem is None:
            self.cnt[eng] += inc
            return (eng, self.cnt[eng])
        else:
            self.dma_cnt[dsem] += inc
            return (dsem, self.dma_cnt[dsem])

    def step(self, eng, fn, reads=(), writes=(), inc=1, extra_deps=(), dsem=None,
             reads_reg=()):
        deps = list(extra_deps)
        for b in reads:
            if b.w is not None:
                deps.append(b.w)
        for b in writes:
            if b.w is not None:
                deps.append(b.w)
            deps.extend(b.rs)
        tok = self.em(eng, fn, deps, inc, dsem)
        for b in reads:
            b.rs.append(tok)
        for b in reads_reg:
            b.rs.append(tok)
        for b in writes:
            b.w = tok
            b.rs = []
        return tok

    @staticmethod
    def merge(tokens):
        best = {}
        for t in tokens:
            if t is None:
                continue
            if t[1] > best.get(t[0], 0):
                best[t[0]] = t[1]
        return [(k, v) for k, v in best.items()]

    def barrier(self):
        self.dma_sem("BGAT")
        self.dma_sem("BREL")
        dtoks = [(s, self.dma_cnt[s]) for s in self.dma_sems
                 if self.dma_cnt[s] > 0 and s not in ("BGAT", "BREL")]
        for e in ENGS:
            self.em(e, "DRAIN", [], inc=1, dsem="BGAT")
        gat_full = ("BGAT", self.dma_cnt["BGAT"])
        self.em("gpsimd", None, [gat_full] + dtoks, inc=1, dsem="BREL")
        rel = ("BREL", self.dma_cnt["BREL"])
        for e in ENGS:
            if e != "gpsimd":
                self.em(e, None, [rel], inc=0)

    def build(self):
        nc = self.nc
        with ExitStack() as stk:
            sems = {e: stk.enter_context(nc.semaphore(f"S_{e}")) for e in ENGS}
            for s in self.dma_sems:
                sems[s] = stk.enter_context(nc.semaphore(f"D_{s}"))
            with nc.Block() as block:
                for e in ENGS:
                    def mk(e):
                        def body(eng):
                            hw = {}
                            for deps, fn, inc, dsem in self.steps[e]:
                                for (pe, pv) in deps:
                                    if pv > hw.get(pe, 0):
                                        eng.wait_ge(sems[pe], pv)
                                        hw[pe] = pv
                                if fn == "DRAIN":
                                    inst = eng.drain()
                                elif fn is not None:
                                    inst = fn(eng)
                                else:
                                    inst = eng.nop()
                                if inc:
                                    inst.then_inc(sems[e if dsem is None else dsem], inc)
                        return body
                    getattr(block, e)(mk(e))


def ceil_div(a, b):
    return (a + b - 1) // b



def build_program(debug_taps=None):
    nc = bass.Bass("TRN2", target_bir_lowering=False)
    P = Prog(nc)

    x_in = nc.dram_tensor("x", [NSEQ, T, D], F32, kind="ExternalInput").ap()
    out_t = nc.dram_tensor("out", [NSEQ, T, D], F32, kind="ExternalOutput").ap()
    W = {}
    for name, shape in [
        ("attn_wq", [D, 1024]), ("attn_wk_down", [D, 64]), ("attn_wv_down", [D, 64]),
        ("attn_wk_up", [32, 128]), ("attn_wv_up", [32, 128]), ("attn_wo", [1024, D]),
        ("ffn_wqkv", [2, D, 1536]), ("ffn_q_scale", [2, 64]), ("ffn_k_scale", [2, 64]),
        ("ffn_wo", [2, 512, D]), ("ffn_ln_g", [2, D]), ("ffn_ln_b", [2, D]),
        ("ffn_w1", [2, D, 5460]), ("ffn_b1", [2, 5460]), ("ffn_w2", [2, HID, D]),
        ("ffn_b2", [2, D]),
        ("ffn_qsr", [2, 128, 1]), ("ffn_ksr", [2, 128, 1]),
        ("ffn_b1ar", [2, 128, 22]), ("ffn_b1gr", [2, 128, 22]),
        ("ffn_lngr", [2, 128, 8]), ("ffn_lnbr", [2, 128, 8]), ("ffn_b2r", [2, 128, 8]),
        ("attn_wkdr", [128, 2048]), ("attn_wvdr", [128, 512]),
    ]:
        W[name] = nc.dram_tensor(name, shape, F32, kind="ExternalInput").ap()
    identc = nc.dram_tensor("identc", [128, 128], F32, kind="ExternalInput").ap()
    protT = nc.dram_tensor("protT", [128, 128], F32, kind="ExternalInput").ap()
    tri_up = nc.dram_tensor("tri_up", [128, 128], F32, kind="ExternalInput").ap()
    tri_lo = nc.dram_tensor("tri_lo", [128, 128], F32, kind="ExternalInput").ap()
    ind2c = nc.dram_tensor("ind2c", [2, 128], F32, kind="ExternalInput").ap()
    tabc = nc.dram_tensor("tabc", [4, 128, 1], F32, kind="ExternalInput").ap()

    park_xn = nc.dram_tensor("park_xn", [NSEQ, 128, NCH * CW], F32).ap()
    park_tab = nc.dram_tensor("park_tab", [4, 128, T], F32).ap()
    park_ab = nc.dram_tensor("park_ab", [2, 128, 2048], F32).ap()
    w1s = nc.dram_tensor("w1s", [2, 2, D, 5460], BF16).ap()
    w2s = nc.dram_tensor("w2s", [2, 2, HID, D], BF16).ap()
    wqkvs = nc.dram_tensor("wqkvs", [2, 2, D, 1536], BF16).ap()
    wofs = nc.dram_tensor("wofs", [2, 2, 512, D], BF16).ap()

    env = {}
    taps = dict(debug_taps or [])

    with ExitStack() as G:
        def sb(name, p, f, dt=F32):
            return G.enter_context(nc.sbuf_tensor(name, [p, f], dt))

        ident = Buf(sb("ident", 128, 128).ap())
        prot = Buf(sb("prot", 128, 128).ap())
        triu = Buf(sb("triu", 128, 128).ap())
        tril = Buf(sb("tril", 128, 128).ap())
        onesm = Buf(sb("onesm", 128, 128).ap())
        h2 = Buf(sb("h2", 128, 2).ap())
        ind2 = Buf(sb("ind2", 2, 128).ap())
        stream_t = sb("stream", 128, NCH * CW)
        stream = [Buf(stream_t[:, c * CW: (c + 1) * CW]) for c in range(NCH)]
        env["stream"] = stream
        env["stream_t"] = stream_t

        csem = P.dma_sem("consts")
        P.em("sync", lambda e: e.dma_start(out=ident.ap, in_=identc[:, :]), [], 16, csem)
        P.em("sync", lambda e: e.dma_start(out=prot.ap, in_=protT[:, :]), [], 16, csem)
        P.em("sync", lambda e: e.dma_start(out=triu.ap, in_=tri_up[:, :]), [], 16, csem)
        P.em("sync", lambda e: e.dma_start(out=tril.ap, in_=tri_lo[:, :]), [], 16, csem)
        P.em("sync", lambda e: e.dma_start(out=ind2.ap, in_=ind2c[:, :]), [], 16, csem)
        for b in (ident, prot, triu, tril, ind2):
            b.w = (csem, 80)
        P.step("vector", lambda e: e.memset(onesm.ap, 1.0), writes=[onesm])
        P.step("vector", lambda e: e.memset(h2.ap, 0.0), writes=[h2])
        P.step("vector", lambda e: e.memset(h2.ap[0:64, 0:1], 1.0), reads=[h2], writes=[h2])
        P.step("vector", lambda e: e.memset(h2.ap[64:128, 1:2], 1.0), reads=[h2], writes=[h2])

        env.update(ident=ident, prot=prot, triu=triu, tril=tril, onesm=onesm,
                   h2=h2, ind2=ind2)

        env["tab_consts_dram"] = tabc
        with ExitStack() as TG:
            build_tables(nc, P, TG, env, park_tab)
            if "tables" in taps:
                taps["tables"](nc, P, env)
            P.barrier()

        precompute_AB(nc, P, G, env, W, park_ab)

        split_weights(nc, P, G, env, W, w1s, w2s, wqkvs, wofs)
        env["w1s"], env["w2s"], env["wqkvs"] = w1s, w2s, wqkvs
        env["wofs"] = wofs

        env["tapBattn"] = taps.get("Battn")
        for s in range(NSEQ):
            phase_A(nc, P, G, env, x_in, park_xn, s)
            if s == 0 and "A" in taps:
                taps["A"](nc, P, env)
            P.barrier()
            for l in range(2):
                phase_B_layer(nc, P, G, env, W, park_tab, s, l)
                if s == 0 and f"B{l}" in taps:
                    taps[f"B{l}"](nc, P, env)
                P.barrier()
            phase_C(nc, P, G, env, W, park_xn, park_ab, s)
            if s == 0 and "C" in taps:
                taps["C"](nc, P, env)
            P.barrier()
            phase_D(nc, P, G, env, x_in, out_t, s)
            P.barrier()

        P.barrier()

    P.build()
    return nc, env


def build_tables(nc, P, G, env, park_tab):
    ts = lambda n, p, f, dt=F32: G.enter_context(nc.sbuf_tensor(n, [p, f], dt))
    invfh = Buf(ts("tb_invfh", 128, 1).ap())
    invfl = Buf(ts("tb_invfl", 128, 1).ap())
    lnbh = Buf(ts("tb_lnbh", 128, 1).ap())
    lnbl = Buf(ts("tb_lnbl", 128, 1).ap())
    trow = Buf(ts("tb_trow", 128, T).ap())
    freq = Buf(ts("tb_freq", 128, T).ap())
    kf = Buf(ts("tb_kf", 128, T).ap())
    ki = Buf(ts("tb_ki", 128, T, I32).ap())
    r = Buf(ts("tb_r", 128, T).ap())
    cmp = Buf(ts("tb_cmp", 128, T).ap())
    rw = Buf(ts("tb_rw", 128, T).ap())
    sinr = Buf(ts("tb_sin", 128, T).ap())
    cosr = Buf(ts("tb_cos", 128, T).ap())
    scl = Buf(ts("tb_scl", 128, T).ap())
    sclin = Buf(ts("tb_sclin", 128, T).ap())
    tabs = Buf(ts("tb_tab", 128, T).ap())
    env["tab_bufs"] = (sinr, cosr, scl, sclin)
    env["ptab_sem"] = P.dma_sem("ptab")

    V, S, GP = "vector", "scalar", "gpsimd"
    tcsem = P.dma_sem("tabconsts")
    tc = env["tab_consts_dram"]
    for i, b in enumerate((invfh, invfl, lnbh, lnbl)):
        P.em("sync", lambda e, i=i, b=b: e.dma_start(out=b.ap, in_=tc[i]), [], 16, tcsem)
    for b in (invfh, invfl, lnbh, lnbl):
        b.w = (tcsem, 64)
    P.step(GP, lambda e: e.iota(ki.ap, [[1, T]], base=0, channel_multiplier=0),
           reads=[], writes=[ki])
    P.step(V, lambda e: e.tensor_copy(trow.ap, ki.ap), reads=[ki], writes=[trow])
    P.step(V, lambda e: e.tensor_scalar(freq.ap, trow.ap, invfh.ap, None, op0=OP.mult),
           reads=[trow, invfh], writes=[freq])
    P.step(V, lambda e: e.tensor_scalar(cmp.ap, trow.ap, invfl.ap, None, op0=OP.mult),
           reads=[trow, invfl], writes=[cmp])
    P.step(V, lambda e: e.tensor_tensor(freq.ap, freq.ap, cmp.ap, op=OP.add),
           reads=[freq, cmp], writes=[freq])
    P.step(V, lambda e: e.tensor_scalar(kf.ap, freq.ap, 1.0 / TWO_PI, None, op0=OP.mult),
           reads=[freq], writes=[kf])
    P.step(V, lambda e: e.tensor_copy(ki.ap, kf.ap), reads=[kf], writes=[ki])
    P.step(V, lambda e: e.tensor_scalar(kf.ap, ki.ap, -1.0, None, op0=OP.mult),
           reads=[ki], writes=[kf])
    for c in (CW_C1, CW_C2, CW_C3):
        P.step(V, lambda e, c=c: e.scalar_tensor_tensor(
            r.ap, kf.ap, c, freq.ap if c == CW_C1 else r.ap, OP.mult, OP.add),
            reads=[kf, freq, r] if c != CW_C1 else [kf, freq], writes=[r])
    P.step(V, lambda e: e.tensor_scalar(cmp.ap, r.ap, PI_F32, None, op0=OP.is_gt),
           reads=[r], writes=[cmp])
    P.step(V, lambda e: e.scalar_tensor_tensor(r.ap, cmp.ap, -TWO_PI, r.ap, OP.mult, OP.add),
           reads=[cmp, r], writes=[r])
    P.step(V, lambda e: e.tensor_scalar(cmp.ap, r.ap, -PI_F32, None, op0=OP.is_lt),
           reads=[r], writes=[cmp])
    P.step(V, lambda e: e.scalar_tensor_tensor(r.ap, cmp.ap, TWO_PI, r.ap, OP.mult, OP.add),
           reads=[cmp, r], writes=[r])
    P.step(V, lambda e: e.tensor_scalar(r.ap, r.ap, PI_F32, -PI_F32, op0=OP.min, op1=OP.max),
           reads=[r], writes=[r])
    P.step(V, lambda e: e.tensor_scalar(rw.ap, r.ap, math.pi / 2, None, op0=OP.add),
           reads=[r], writes=[rw])
    P.step(V, lambda e: e.tensor_scalar(cmp.ap, rw.ap, PI_F32, None, op0=OP.is_gt),
           reads=[rw], writes=[cmp])
    P.step(V, lambda e: e.scalar_tensor_tensor(rw.ap, cmp.ap, -TWO_PI, rw.ap, OP.mult, OP.add),
           reads=[cmp, rw], writes=[rw])
    P.step(V, lambda e: e.tensor_scalar(rw.ap, rw.ap, PI_F32, -PI_F32, op0=OP.min, op1=OP.max),
           reads=[rw], writes=[rw])
    P.step(S, lambda e: e.activation(out=sinr.ap, in_=r.ap, func=AF.Sin),
           reads=[r], writes=[sinr])
    P.step(S, lambda e: e.activation(out=cosr.ap, in_=rw.ap, func=AF.Sin),
           reads=[rw], writes=[cosr])
    P.step(V, lambda e: e.tensor_scalar(cmp.ap, trow.ap, 1.0 / 256.0, -4.0,
                                        op0=OP.mult, op1=OP.add),
           reads=[trow], writes=[cmp])
    P.step(V, lambda e: e.tensor_scalar(scl.ap, cmp.ap, lnbh.ap, None, op0=OP.mult),
           reads=[cmp, lnbh], writes=[scl])
    P.step(V, lambda e: e.tensor_scalar(sclin.ap, cmp.ap, lnbl.ap, None, op0=OP.mult),
           reads=[cmp, lnbl], writes=[sclin])
    P.step(V, lambda e: e.tensor_tensor(scl.ap, scl.ap, sclin.ap, op=OP.add),
           reads=[scl, sclin], writes=[scl])
    P.step(V, lambda e: e.tensor_scalar(sclin.ap, scl.ap, -1.0, None, op0=OP.mult),
           reads=[scl], writes=[sclin])
    P.step(S, lambda e: e.activation(out=scl.ap, in_=scl.ap, func=AF.Exp),
           reads=[scl], writes=[scl])
    P.step(S, lambda e: e.activation(out=sclin.ap, in_=sclin.ap, func=AF.Exp),
           reads=[sclin], writes=[sclin])
    for i, (trig, sc) in enumerate([(cosr, scl), (sinr, scl), (cosr, sclin), (sinr, sclin)]):
        P.step(V, lambda e, trig=trig, sc=sc: e.tensor_tensor(tabs.ap, trig.ap, sc.ap,
                                                              op=OP.mult),
               reads=[trig, sc], writes=[tabs])
        P.step(GP, lambda e, i=i: e.dma_start(out=park_tab[i], in_=tabs.ap),
               reads=[tabs], writes=[], inc=16, dsem=env["ptab_sem"])


def emit_l2norm_rotary(nc, P, env, psum_in, tgt_ap, tgt_buf, scale_vec, tabc, tabs,
                       aux, pfx):
    h2, ind2, prot = env["h2"], env["ind2"], env["prot"]
    sq, nrm, bc_sb, qn, u = aux["sq"], aux["nrm"], aux["bc_sb"], aux["qn"], aux["u"]
    p_ssq, p_bc, p_rot = aux["p_ssq"], aux["p_bc"], aux["p_rot"]
    P.step("scalar", lambda e: e.activation(out=sq.ap, in_=psum_in.ap, func=AF.Square),
           reads=[psum_in], writes=[sq])
    P.step("tensor", lambda e: e.matmul(p_ssq.ap, h2.ap, sq.ap, start=True, stop=True),
           reads=[h2, sq], writes=[p_ssq])
    P.step("scalar", lambda e: e.activation(out=nrm.ap, in_=p_ssq.ap, func=AF.Sqrt),
           reads=[p_ssq], writes=[nrm])
    P.step("vector", lambda e: e.tensor_scalar(nrm.ap, nrm.ap, 1e-12, None, op0=OP.max),
           reads=[nrm], writes=[nrm])
    P.step("vector", lambda e: e.reciprocal(nrm.ap, nrm.ap), reads=[nrm], writes=[nrm])
    P.step("tensor", lambda e: e.matmul(p_bc.ap, ind2.ap, nrm.ap, start=True, stop=True),
           reads=[ind2, nrm], writes=[p_bc])
    P.step("vector", lambda e: e.tensor_copy(bc_sb.ap, p_bc.ap), reads=[p_bc], writes=[bc_sb])
    P.step("vector", lambda e: e.scalar_tensor_tensor(qn.ap, psum_in.ap, scale_vec.ap,
                                                      bc_sb.ap, OP.mult, OP.mult),
           reads=[psum_in, scale_vec, bc_sb], writes=[qn])
    P.step("tensor", lambda e: e.matmul(p_rot.ap, prot.ap, qn.ap, start=True, stop=True),
           reads=[prot, qn], writes=[p_rot])
    P.step("vector", lambda e: e.tensor_tensor(u.ap, qn.ap, tabc, op=OP.mult),
           reads=[qn] + aux["tab_reads"], writes=[u])
    P.step("vector", lambda e: e.tensor_tensor(qn.ap, p_rot.ap, tabs, op=OP.mult),
           reads=[p_rot] + aux["tab_reads"], writes=[qn])
    P.step("vector", lambda e: e.tensor_tensor(tgt_ap, u.ap, qn.ap, op=OP.add),
           reads=[u, qn], writes=[tgt_buf])


def phase_B_layer(nc, P, G, env, W, park_tab, s, l):
    stream = env["stream"]
    onesm, triu, tril, ident = env["onesm"], env["triu"], env["tril"], env["ident"]
    wqkv = W["ffn_wqkv"][l].rearrange("(a p) n -> p a n", p=128)
    w1r = W["ffn_w1"][l].rearrange("(a p) n -> p a n", p=128)
    w2r = W["ffn_w2"][l]
    wor = W["ffn_wo"][l]

    def mm_acc(pb, items, region=None):
        fd = ([pb.w] if pb.w else []) + pb.rs
        for _, _, rd, _ in items:
            fd += [b.w for b in rd]
        fd = Prog.merge(fd)
        n = len(items)
        for i, (lhs, rhs, rd, osl) in enumerate(items):
            out = pb.ap if osl is None else osl
            P.step("tensor", lambda e, lhs=lhs, rhs=rhs, out=out, st=(i == 0),
                   sp=(i == n - 1): e.matmul(out, lhs, rhs, start=st, stop=sp),
                   reads_reg=rd, writes=[], extra_deps=fd if i == 0 else [])
        pb.w = ("tensor", P.cnt["tensor"])
        pb.rs = []

    with ExitStack() as LA:
        ts = lambda n, p, f, dt=F32: LA.enter_context(
            nc.sbuf_tensor(f"B{s}{l}_{n}", [p, f], dt))
        pool = [Buf(LA.enter_context(
            nc.psum_tensor(f"B{s}{l}_pp{i}", [128, 512], F32)).ap()) for i in range(8)]
        tabs4 = [Buf(ts(f"tab{i}", 128, 512).ap()) for i in range(4)]
        tbsem = [P.dma_sem(f"Btab{i}") for i in range(4)]
        ptok = (env["ptab_sem"], P.dma_cnt[env["ptab_sem"]])

        def load_tables(w):
            for i in range(4):
                P.step("sync", lambda e, i=i, w=w: e.dma_start(
                    out=tabs4[i].ap, in_=park_tab[i][:, 512 * w:512 * (w + 1)]),
                    reads=[], writes=[tabs4[i]], inc=16, dsem=tbsem[i],
                    extra_deps=[ptok])
        vsem = P.dma_sem("Bvec")
        qsv = Buf(ts("qsv", 128, 1).ap())
        ksv = Buf(ts("ksv", 128, 1).ap())
        P.step("sync", lambda e: e.dma_start(out=qsv.ap, in_=W["ffn_qsr"][l]),
               reads=[], writes=[qsv], inc=16, dsem=vsem)
        P.step("sync", lambda e: e.dma_start(out=ksv.ap, in_=W["ffn_ksr"][l]),
               reads=[], writes=[ksv], inc=16, dsem=vsem)
        vtok = (vsem, P.dma_cnt[vsem])
        qsv.w = vtok
        ksv.w = vtok

        qf = Buf(ts("qf", 128, 4 * 512).ap())
        kf = Buf(ts("kf", 128, 8 * 512).ap())
        vt = Buf(ts("vt", 128, 8 * 520).ap())
        probsT = Buf(ts("probsT", 128, 8 * 512).ap())
        y_w = Buf(ts("y_w", 64, 512).ap())
        aux2 = [{
            "sq": Buf(ts(f"sq{p}", 128, 512).ap()),
            "nrm": Buf(ts(f"nrm{p}", 2, 512).ap()),
            "bc_sb": Buf(ts(f"bc_sb{p}", 128, 512).ap()),
            "qn": Buf(ts(f"qn{p}", 128, 512).ap()),
            "u": Buf(ts(f"u{p}", 128, 512).ap()),
        } for p in range(2)]
        rec_t = ts("rec", 128, 512)
        rec = Buf(rec_t.ap()[64:65, :])
        ybc = Buf(ts("ybc", 64, 512).ap())
        tsb = lambda n, p, f: LA.enter_context(
            nc.sbuf_tensor(f"B{s}{l}_{n}", [p, f], BF16))
        wsem = [P.dma_sem(f"Bwq{i}") for i in range(4)]
        wqb = [(Buf(tsb(f"wqbh{i}", 128, 8 * 128).ap()),
                Buf(tsb(f"wqbl{i}", 128, 8 * 128).ap())) for i in range(2)]
        hh_b = Buf(tsb("hh", 128, 8 * 512).ap())
        hl_b = Buf(tsb("hl", 128, 8 * 512).ap())
        htmp = Buf(ts("htmp", 128, 512).ap())
        wqv = [env["wqkvs"][l][i].rearrange("(a p) n -> p a n", p=128) for i in range(2)]
        wosem = [P.dma_sem(f"Bwo{i}") for i in range(4)]
        wob = [(Buf(tsb(f"wobh{i}", 64, 1024).ap()),
                Buf(tsb(f"wobl{i}", 64, 1024).ap())) for i in range(2)]
        ywh = Buf(tsb("ywh", 64, 8 * 512).ap())
        ywl = Buf(tsb("ywl", 64, 8 * 512).ap())
        wofr = env["wofs"][l]
        vt_view = vt.ap.rearrange("p (sl r) -> p sl r", r=520)
        ones_cols = vt.ap.rearrange("p (sl h c) -> p (sl h) c", h=8, c=65)[:, :, 64:65]
        P.step("vector", lambda e: e.memset(ones_cols, 1.0), reads=[], writes=[vt])

        def qkv_window(w):
            sl = w % 2
            load_tables(w)
            for dc in range(NCH):
                shift = 1 if dc < 4 else 0
                h_ap = stream[dc].ap[:, shift + 512 * w: shift + 512 * (w + 1)]
                P.step("vector", lambda e, h_ap=h_ap, dc=dc: e.tensor_copy(
                    hh_b.ap[:, 512 * dc:512 * (dc + 1)], h_ap),
                    reads=[stream[dc]], writes=[hh_b])
                P.step("vector", lambda e, h_ap=h_ap, dc=dc: e.tensor_tensor(
                    hl_b.ap[:, 512 * dc:512 * (dc + 1)], h_ap,
                    hh_b.ap[:, 512 * dc:512 * (dc + 1)], op=OP.subtract),
                    reads=[stream[dc], hh_b], writes=[hl_b])
            for oc in range(12):
                wbh, wbl = wqb[oc % 2]
                P.step("sync", lambda e, oc=oc, wbh=wbh: e.dma_start(
                    out=wbh.ap.rearrange("p (a n) -> p a n", n=128),
                    in_=wqv[0][:, :, 128 * oc:128 * (oc + 1)]),
                    reads=[], writes=[wbh], inc=16, dsem=wsem[oc % 2])
                P.step("sync", lambda e, oc=oc, wbl=wbl: e.dma_start(
                    out=wbl.ap.rearrange("p (a n) -> p a n", n=128),
                    in_=wqv[1][:, :, 128 * oc:128 * (oc + 1)]),
                    reads=[], writes=[wbl], inc=16, dsem=wsem[2 + oc % 2])
                pq = pool[oc % 2]
                items = []
                for dc in range(NCH):
                    wh = wbh.ap[:, 128 * dc:128 * (dc + 1)]
                    wl = wbl.ap[:, 128 * dc:128 * (dc + 1)]
                    xh = hh_b.ap[:, 512 * dc:512 * (dc + 1)]
                    xl = hl_b.ap[:, 512 * dc:512 * (dc + 1)]
                    items.append((wh, xh, [wbh, hh_b], None))
                    items.append((wh, xl, [wbh, hl_b], None))
                    items.append((wl, xh, [wbl, hh_b], None))
                mm_acc(pq, items)
                if oc < 8:
                    pr = oc % 4
                    is_q = oc < 4
                    sv = qsv if is_q else ksv
                    tc = tabs4[0] if is_q else tabs4[2]
                    tsn = tabs4[1] if is_q else tabs4[3]
                    if is_q:
                        tgt_ap = qf.ap[:, 512 * pr: 512 * (pr + 1)]
                        tgt_buf = qf
                    else:
                        tgt_ap = kf.ap[:, (pr * 2 + sl) * 512: (pr * 2 + sl + 1) * 512]
                        tgt_buf = kf
                    par = oc % 2
                    p_ssq, p_bc, p_rot = pool[2 + 3 * par], pool[3 + 3 * par], pool[4 + 3 * par]
                    aux = aux2[par]
                    sq, nrm, bc_sb, qn, u = (aux["sq"], aux["nrm"], aux["bc_sb"],
                                             aux["qn"], aux["u"])
                    h2, ind2, prot = env["h2"], env["ind2"], env["prot"]
                    P.step("scalar", lambda e, pq=pq, sq=sq: e.activation(
                        out=sq.ap, in_=pq.ap, func=AF.Square),
                        reads=[pq], writes=[sq])
                    mm_acc(p_ssq, [(h2.ap, sq.ap, [h2, sq], p_ssq.ap[0:2, :])])
                    P.step("vector", lambda e, nrm=nrm, p_ssq=p_ssq: e.tensor_scalar(
                        nrm.ap, p_ssq.ap[0:2, :], 1e-24, None, op0=OP.max),
                        reads=[p_ssq, nrm], writes=[nrm])
                    P.step("scalar", lambda e, nrm=nrm: e.activation(
                        out=nrm.ap, in_=nrm.ap, func=AF.Ln),
                        reads=[nrm], writes=[nrm])
                    P.step("scalar", lambda e, nrm=nrm: e.activation(
                        out=nrm.ap, in_=nrm.ap, func=AF.Exp, scale=-0.5),
                        reads=[nrm], writes=[nrm])
                    mm_acc(p_bc, [(ind2.ap, nrm.ap, [ind2, nrm], None)])
                    P.step("scalar", lambda e, bc_sb=bc_sb, p_bc=p_bc: e.copy(
                        bc_sb.ap, p_bc.ap),
                        reads=[p_bc], writes=[bc_sb])
                    P.step("vector", lambda e, pq=pq, qn=qn, sv=sv: e.tensor_scalar(
                        qn.ap, pq.ap, sv.ap, None, op0=OP.mult),
                        reads=[pq, sv], writes=[qn])
                    mm_acc(p_rot, [(prot.ap, qn.ap, [prot, qn], None)])
                    tc_ap = tc.ap
                    ts_ap = tsn.ap
                    P.step("vector", lambda e, tc_ap=tc_ap, u=u, qn=qn: e.tensor_tensor(
                        u.ap, qn.ap, tc_ap, op=OP.mult),
                        reads=[qn, tc], writes=[u])
                    P.step("vector", lambda e, ts_ap=ts_ap, qn=qn, p_rot=p_rot:
                           e.tensor_tensor(
                        qn.ap, p_rot.ap, ts_ap, op=OP.mult),
                        reads=[p_rot, tsn], writes=[qn])
                    P.step("vector", lambda e, u=u, qn=qn: e.tensor_tensor(
                        u.ap, u.ap, qn.ap, op=OP.add),
                        reads=[u, qn], writes=[u])
                    P.step("vector", lambda e, tgt_ap=tgt_ap, u=u, bc_sb=bc_sb:
                           e.tensor_tensor(
                        tgt_ap, u.ap, bc_sb.ap, op=OP.mult),
                        reads=[u, bc_sb], writes=[tgt_buf])
                else:
                    pr = oc - 8
                    p_vt = pool[2 + 3 * (oc % 2)]
                    vfm = aux2[oc % 2]["sq"]
                    P.step("vector", lambda e, pq=pq, vfm=vfm: e.tensor_copy(
                        vfm.ap, pq.ap),
                        reads=[pq], writes=[vfm])
                    fd = ([p_vt.w] if p_vt.w else []) + p_vt.rs
                    for tt in range(4):
                        P.step("tensor", lambda e, tt=tt, p_vt=p_vt, vfm=vfm: e.transpose(
                            p_vt.ap[:, 128 * tt:128 * (tt + 1)],
                            vfm.ap[:, 128 * tt:128 * (tt + 1)], ident.ap),
                            reads=[vfm, ident], writes=[],
                            extra_deps=fd if tt == 0 else [])
                    p_vt.w = ("tensor", P.cnt["tensor"])
                    p_vt.rs = []
                    dst = vt.ap.rearrange("p (sl h c) -> p sl h c", h=8, c=65)[
                        :, sl * 4:sl * 4 + 4, 2 * pr:2 * pr + 2, 0:64]
                    src = p_vt.ap.rearrange("p (tt h c) -> p tt h c", h=2, c=64)
                    P.step("vector", lambda e, dst=dst, src=src: e.tensor_copy(dst, src),
                           reads=[p_vt], writes=[vt])

        def attn_window(w):
            sl = w % 2
            psl = (w - 1) % 2
            for h in range(L_HEADS):
                pr, hh = h // 2, h % 2
                base = 64 * hh
                kts = list(range(4, 8)) if w == 0 else list(range(8))
                for kt in kts:
                    qlo = max(0, 128 * kt - 512)
                    qhi = min(512, 128 * kt + 128)
                    slot = psl if kt < 4 else sl
                    ksub = kt % 4
                    lhs = kf.ap[base:base + 64,
                                (pr * 2 + slot) * 512 + 128 * ksub:
                                (pr * 2 + slot) * 512 + 128 * (ksub + 1)]
                    rhs = qf.ap[base:base + 64, 512 * pr + qlo: 512 * pr + qhi]
                    pc = pool[kt % 2]
                    mm_acc(pc, [(lhs, rhs, [kf, qf], pc.ap[:, 0:qhi - qlo])])
                    if kt >= 4:
                        mask, mlo = tril, 0
                    else:
                        mask, mlo = triu, (qhi - qlo) - 128
                    P.step("vector", lambda e, pc=pc, mask=mask, mlo=mlo:
                           e.tensor_tensor(pc.ap[:, mlo:mlo + 128],
                                           pc.ap[:, mlo:mlo + 128], mask.ap, op=OP.add),
                           reads=[pc, mask], writes=[pc])
                    P.step("scalar", lambda e, pc=pc, kt=kt, qlo=qlo, qhi=qhi:
                           e.activation(out=probsT.ap[:, 512 * kt + qlo: 512 * kt + qhi],
                                        in_=pc.ap[:, 0:qhi - qlo], func=AF.Exp,
                                        scale=QK_SCALE),
                           reads=[pc], writes=[probsT])
                korder = [4] + [kt for kt in kts if kt != 4]
                p_y = pool[2]
                items_y = []
                for i, kt in enumerate(korder):
                    qlo = max(0, 128 * kt - 512)
                    qhi = min(512, 128 * kt + 128)
                    slot = psl if kt < 4 else sl
                    ksub = kt % 4
                    vl = vt.ap[:, (slot * 4 + ksub) * 520 + 65 * h:
                               (slot * 4 + ksub) * 520 + 65 * (h + 1)]
                    pT = probsT.ap[:, 512 * kt + qlo: 512 * kt + qhi]
                    items_y.append((vl, pT, [vt, probsT], p_y.ap[0:65, qlo:qhi]))
                mm_acc(p_y, items_y)
                p_yb = pool[4]
                P.step("scalar", lambda e, p_y=p_y: e.activation(
                    out=rec.ap, in_=p_y.ap[64:65, :], func=AF.Ln),
                    reads=[p_y, rec], writes=[rec])
                P.step("scalar", lambda e: e.activation(
                    out=rec.ap, in_=rec.ap, func=AF.Exp, scale=-1.0),
                    reads=[rec], writes=[rec])
                mm_acc(p_yb, [(onesm.ap[64:65, 0:64], rec.ap, [onesm, rec],
                               p_yb.ap[0:64, :])])
                P.step("vector", lambda e, p_yb=p_yb: e.tensor_copy(ybc.ap,
                                                                    p_yb.ap[0:64, :]),
                       reads=[p_yb], writes=[ybc])
                P.step("vector", lambda e, p_y=p_y: e.tensor_tensor(
                    y_w.ap, p_y.ap[0:64, :], ybc.ap, op=OP.mult),
                    reads=[p_y, ybc], writes=[y_w])
                P.step("scalar", lambda e, h=h: e.copy(
                    ywh.ap[:, 512 * h: 512 * (h + 1)], y_w.ap),
                    reads=[y_w, ywh], writes=[ywh])
                P.step("vector", lambda e, h=h: e.tensor_tensor(
                    ywl.ap[:, 512 * h: 512 * (h + 1)], y_w.ap,
                    ywh.ap[:, 512 * h: 512 * (h + 1)], op=OP.subtract),
                    reads=[y_w, ywh, ywl], writes=[ywl])

        def wo_window(w):
            for h in range(L_HEADS):
                wbh, wbl = wob[h % 2]
                P.step("sync", lambda e, h=h, wbh=wbh: e.dma_start(
                    out=wbh.ap, in_=wofr[0][64 * h: 64 * (h + 1), :]),
                    reads=[], writes=[wbh], inc=16, dsem=wosem[h % 2])
                P.step("sync", lambda e, h=h, wbl=wbl: e.dma_start(
                    out=wbl.ap, in_=wofr[1][64 * h: 64 * (h + 1), :]),
                    reads=[], writes=[wbl], inc=16, dsem=wosem[2 + h % 2])
                for dc in range(NCH):
                    pw = pool[dc]
                    fd = Prog.merge(([pw.w] if (h == 0 and pw.w) else []) +
                                    (pw.rs if h == 0 else []) +
                                    [wbh.w, wbl.w, ywh.w, ywl.w])
                    triples = ((wbh, ywh), (wbh, ywl), (wbl, ywh))
                    for ti, (wb_, yw_) in enumerate(triples):
                        st = (h == 0 and ti == 0)
                        sp = (h == L_HEADS - 1 and ti == 2)
                        P.step("tensor", lambda e, wb_=wb_, yw_=yw_, dc=dc, h=h, pw=pw,
                               st=st, sp=sp:
                               e.matmul(pw.ap, wb_.ap[:, 128 * dc:128 * (dc + 1)],
                                        yw_.ap[:, 512 * h: 512 * (h + 1)],
                                        start=st, stop=sp),
                               reads_reg=[wb_, yw_], writes=[],
                               extra_deps=fd if ti == 0 else [])
                    if h == L_HEADS - 1:
                        pw.w = ("tensor", P.cnt["tensor"])
                        pw.rs = []
            for dc in range(NCH):
                pw = pool[dc]
                dst = stream[dc].ap[:, 1 + 512 * w: 1 + 512 * (w + 1)]
                P.step("vector", lambda e, dst=dst, pw=pw: e.tensor_tensor(
                    dst, dst, pw.ap, op=OP.add),
                    reads=[pw, stream[dc]], writes=[stream[dc]])

        qkv_window(0)
        for w in range(1, NWIN):
            attn_window(w - 1)
            qkv_window(w)
            wo_window(w - 1)
        attn_window(NWIN - 1)
        wo_window(NWIN - 1)
        if env.get("tapBattn") and s == 0 and l == 0:
            env["tapBattn"](nc, P, env, locals())

    P.barrier()
    with ExitStack() as LM:
        ts = lambda n, p, f, dt=F32: LM.enter_context(
            nc.sbuf_tensor(f"M{s}{l}_{n}", [p, f], dt))
        pool = [Buf(LM.enter_context(
            nc.psum_tensor(f"M{s}{l}_pp{i}", [128, 512], F32)).ap()) for i in range(8)]
        tsb = lambda n, p, f: LM.enter_context(
            nc.sbuf_tensor(f"M{s}{l}_{n}", [p, f], BF16))
        lnout = [(Buf(tsb(f"lnoh{i}", 128, NCH * 512).ap()),
                  Buf(tsb(f"lnol{i}", 128, NCH * 512).ap())) for i in range(2)]
        gg = (Buf(tsb("ggh", 128, 22 * 512).ap()), Buf(tsb("ggl", 128, 22 * 512).ap()))
        w1b = [(Buf(tsb(f"w1bh{i}", 128, 8 * 512).ap()),
                Buf(tsb(f"w1bl{i}", 128, 8 * 512).ap())) for i in range(2)]
        w1sem = [P.dma_sem(f"Mw1{i}") for i in range(4)]
        w2b = [(Buf(tsb(f"w2bh{i}", 128, 1024).ap()),
                Buf(tsb(f"w2bl{i}", 128, 1024).ap())) for i in range(2)]
        w2sem = [P.dma_sem(f"Mw2{i}") for i in range(4)]
        tmp2 = Buf(ts("tmp2", 128, 512).ap())
        ggt = Buf(ts("ggt", 128, 512).ap())
        w1sr = env["w1s"][l]
        w2sr = env["w2s"][l]
        w1v = [w1sr[i].rearrange("(a p) n -> p a n", p=128) for i in range(2)]
        vecs = Buf(ts("vecs", 128, 68).ap())
        b1a = Buf(vecs.ap[:, 0:22])
        b1g = Buf(vecs.ap[:, 22:44])
        lng = Buf(vecs.ap[:, 44:52])
        lnb = Buf(vecs.ap[:, 52:60])
        b2v = Buf(vecs.ap[:, 60:68])
        mvsem = P.dma_sem("Mv")
        for b, key in [(b1a, "ffn_b1ar"), (b1g, "ffn_b1gr"), (lng, "ffn_lngr"),
                       (lnb, "ffn_lnbr"), (b2v, "ffn_b2r")]:
            P.step("sync", lambda e, b=b, key=key: e.dma_start(out=b.ap, in_=W[key][l]),
                   reads=[], writes=[b], inc=16, dsem=mvsem)
        mtok = (mvsem, P.dma_cnt[mvsem])
        for b in (b1a, b1g, lng, lnb, b2v):
            b.w = mtok
        sqs = Buf(ts("sqs", 128, 512).ap())
        tmp1 = Buf(ts("tmp1", 128, 512).ap())
        gel = Buf(ts("gel", 128, 512).ap())
        mean = Buf(ts("mean", 1, 512).ap())
        rstd = Buf(ts("rstd", 1, 512).ap())

        def emit_ln(tb):
            lo = lnout[tb % 2]
            p_st, p_mb, p_rb = pool[4], pool[5], pool[6]
            items = []
            for dc in range(NCH):
                shift = 1 if dc < 4 else 0
                h_ap = stream[dc].ap[:, shift + 512 * tb: shift + 512 * (tb + 1)]
                items.append((onesm.ap[:, 0:1], h_ap, [onesm, stream[dc]],
                              p_st.ap[0:1, :]))
            mm_acc(p_st, items)
            P.step("vector", lambda e, p_st=p_st: e.tensor_scalar(
                mean.ap, p_st.ap[0:1, :], 1.0 / D, None, op0=OP.mult),
                reads=[p_st], writes=[mean])
            items = []
            for dc in range(NCH):
                shift = 1 if dc < 4 else 0
                h_ap = stream[dc].ap[:, shift + 512 * tb: shift + 512 * (tb + 1)]
                P.step("scalar", lambda e, h_ap=h_ap, dc=dc: e.activation(
                    out=sqs.ap, in_=h_ap, func=AF.Square),
                    reads=[stream[dc]] + ([sqs] if dc == 0 else []), writes=[sqs])
                mm_acc(p_st, [(onesm.ap[:, 0:1], sqs.ap, [onesm, sqs],
                               p_st.ap[0:1, :])]) if dc == NCH - 1 else None
            return lo

        def emit_ln2(tb):
            loh, lol = lnout[tb % 2]
            p_st, p_mb, p_rb = pool[4], pool[5], pool[6]
            items = []
            for dc in range(NCH):
                shift = 1 if dc < 4 else 0
                h_ap = stream[dc].ap[:, shift + 512 * tb: shift + 512 * (tb + 1)]
                items.append((onesm.ap[:, 0:1], h_ap, [onesm, stream[dc]],
                              p_st.ap[0:1, :]))
            mm_acc(p_st, items)
            P.step("vector", lambda e, p_st=p_st: e.tensor_scalar(
                mean.ap, p_st.ap[0:1, :], 1.0 / D, None, op0=OP.mult),
                reads=[p_st], writes=[mean])
            fd = ([p_st.w] if p_st.w else []) + p_st.rs
            for dc in range(NCH):
                shift = 1 if dc < 4 else 0
                h_ap = stream[dc].ap[:, shift + 512 * tb: shift + 512 * (tb + 1)]
                P.step("scalar", lambda e, h_ap=h_ap: e.activation(
                    out=sqs.ap, in_=h_ap, func=AF.Square),
                    reads=[stream[dc], sqs], writes=[sqs])
                P.step("tensor", lambda e, dc=dc: e.matmul(
                    p_st.ap[0:1, :], onesm.ap[:, 0:1], sqs.ap,
                    start=(dc == 0), stop=(dc == NCH - 1)),
                    reads=[onesm, sqs], writes=[],
                    extra_deps=fd if dc == 0 else [])
            p_st.w = ("tensor", P.cnt["tensor"])
            p_st.rs = []
            P.step("scalar", lambda e: e.activation(out=rstd.ap, in_=mean.ap,
                                                    func=AF.Square),
                   reads=[mean], writes=[rstd])
            P.step("vector", lambda e, p_st=p_st: e.scalar_tensor_tensor(
                rstd.ap, p_st.ap[0:1, :], 1.0 / D, rstd.ap, OP.mult, OP.subtract),
                reads=[p_st, rstd], writes=[rstd])
            P.step("vector", lambda e: e.tensor_scalar(rstd.ap, rstd.ap, 1e-5, None,
                                                       op0=OP.add),
                   reads=[rstd], writes=[rstd])
            P.step("scalar", lambda e: e.activation(out=rstd.ap, in_=rstd.ap,
                                                    func=AF.Sqrt),
                   reads=[rstd], writes=[rstd])
            P.step("vector", lambda e: e.reciprocal(rstd.ap, rstd.ap),
                   reads=[rstd], writes=[rstd])
            mm_acc(p_mb, [(onesm.ap[0:1, :], mean.ap, [onesm, mean], None)])
            mm_acc(p_rb, [(onesm.ap[0:1, :], rstd.ap, [onesm, rstd], None)])
            for dc in range(NCH):
                shift = 1 if dc < 4 else 0
                h_ap = stream[dc].ap[:, shift + 512 * tb: shift + 512 * (tb + 1)]
                P.step("vector", lambda e, h_ap=h_ap, p_mb=p_mb: e.tensor_tensor(
                    tmp1.ap, h_ap, p_mb.ap, op=OP.subtract),
                    reads=[stream[dc], p_mb, tmp1], writes=[tmp1])
                P.step("vector", lambda e, p_rb=p_rb: e.tensor_tensor(
                    tmp1.ap, tmp1.ap, p_rb.ap, op=OP.mult),
                    reads=[tmp1, p_rb], writes=[tmp1])
                P.step("vector", lambda e, dc=dc: e.tensor_scalar(
                    tmp2.ap, tmp1.ap, lng.ap[:, dc:dc + 1], lnb.ap[:, dc:dc + 1],
                    op0=OP.mult, op1=OP.add),
                    reads=[tmp1, lng, lnb, tmp2], writes=[tmp2])
                P.step("vector", lambda e, loh=loh, dc=dc: e.tensor_copy(
                    loh.ap[:, 512 * dc: 512 * (dc + 1)], tmp2.ap),
                    reads=[tmp2], writes=[loh])
                P.step("vector", lambda e, loh=loh, lol=lol, dc=dc: e.tensor_tensor(
                    lol.ap[:, 512 * dc: 512 * (dc + 1)], tmp2.ap,
                    loh.ap[:, 512 * dc: 512 * (dc + 1)], op=OP.subtract),
                    reads=[tmp2, loh], writes=[lol])

        def emit_mlp_tb(tb):
            loh, lol = lnout[tb % 2]
            spans = [(sp * 512, min(512, HID - sp * 512))
                     for sp in range(ceil_div(HID, 512))]
            for sp, (c0, cn) in enumerate(spans):
                (abh, abl), (gbh, gbl) = w1b[0], w1b[1]
                P.step("sync", lambda e, c0=c0, cn=cn, abh=abh: e.dma_start(
                    out=abh.ap[:, 0:8 * cn].rearrange("p (a n) -> p a n", n=cn),
                    in_=w1v[0][:, :, c0:c0 + cn]),
                    reads=[], writes=[abh], inc=16, dsem=w1sem[0])
                P.step("sync", lambda e, c0=c0, cn=cn, abl=abl: e.dma_start(
                    out=abl.ap[:, 0:8 * cn].rearrange("p (a n) -> p a n", n=cn),
                    in_=w1v[1][:, :, c0:c0 + cn]),
                    reads=[], writes=[abl], inc=16, dsem=w1sem[1])
                njs = ceil_div(cn, 128)
                for jj in range(njs):
                    m = min(128, cn - 128 * jj)
                    pa = pool[jj]
                    items = []
                    for dc in range(NCH):
                        wh = abh.ap[:, dc * cn + 128 * jj: dc * cn + 128 * jj + m]
                        wl = abl.ap[:, dc * cn + 128 * jj: dc * cn + 128 * jj + m]
                        xh = loh.ap[:, 512 * dc: 512 * (dc + 1)]
                        xl = lol.ap[:, 512 * dc: 512 * (dc + 1)]
                        items.append((wh, xh, [abh, loh], pa.ap[0:m, :]))
                        items.append((wh, xl, [abh, lol], pa.ap[0:m, :]))
                        items.append((wl, xh, [abl, loh], pa.ap[0:m, :]))
                    mm_acc(pa, items)
                P.step("sync", lambda e, c0=c0, cn=cn, gbh=gbh: e.dma_start(
                    out=gbh.ap[:, 0:8 * cn].rearrange("p (a n) -> p a n", n=cn),
                    in_=w1v[0][:, :, HID + c0:HID + c0 + cn]),
                    reads=[], writes=[gbh], inc=16, dsem=w1sem[2])
                P.step("sync", lambda e, c0=c0, cn=cn, gbl=gbl: e.dma_start(
                    out=gbl.ap[:, 0:8 * cn].rearrange("p (a n) -> p a n", n=cn),
                    in_=w1v[1][:, :, HID + c0:HID + c0 + cn]),
                    reads=[], writes=[gbl], inc=16, dsem=w1sem[3])
                for jj in range(njs):
                    j = (c0 + 128 * jj) // 128
                    m = min(128, cn - 128 * jj)
                    pg = pool[4 + jj]
                    items = []
                    for dc in range(NCH):
                        wh = gbh.ap[:, dc * cn + 128 * jj: dc * cn + 128 * jj + m]
                        wl = gbl.ap[:, dc * cn + 128 * jj: dc * cn + 128 * jj + m]
                        xh = loh.ap[:, 512 * dc: 512 * (dc + 1)]
                        xl = lol.ap[:, 512 * dc: 512 * (dc + 1)]
                        items.append((wh, xh, [gbh, loh], pg.ap[0:m, :]))
                        items.append((wh, xl, [gbh, lol], pg.ap[0:m, :]))
                        items.append((wl, xh, [gbl, loh], pg.ap[0:m, :]))
                    mm_acc(pg, items)
                    P.step("scalar", lambda e, pg=pg, j=j, m=m: e.activation(
                        out=gel.ap[0:m, :], in_=pg.ap[0:m, :], func=AF.Gelu,
                        bias=b1g.ap[0:m, j:j + 1]),
                        reads=[pg, b1g, gel], writes=[gel])
                    pa = pool[jj]
                    P.step("vector", lambda e, pa=pa, j=j, m=m: e.scalar_tensor_tensor(
                        ggt.ap[0:m, :], pa.ap[0:m, :],
                        b1a.ap[0:m, j:j + 1], gel.ap[0:m, :], OP.add, OP.mult),
                        reads=[pa, b1a, gel, ggt], writes=[ggt])
                    P.step("vector", lambda e, j=j, m=m: e.tensor_copy(
                        gg[0].ap[0:m, 512 * j: 512 * (j + 1)], ggt.ap[0:m, :]),
                        reads=[ggt], writes=[gg[0]])
                    P.step("vector", lambda e, j=j, m=m: e.tensor_tensor(
                        gg[1].ap[0:m, 512 * j: 512 * (j + 1)], ggt.ap[0:m, :],
                        gg[0].ap[0:m, 512 * j: 512 * (j + 1)], op=OP.subtract),
                        reads=[ggt, gg[0]], writes=[gg[1]])
            njtot = ceil_div(HID, 128)
            fd = {dc: ([pool[dc].w] if pool[dc].w else []) + pool[dc].rs
                  for dc in range(8)}
            for j in range(njtot):
                m = min(128, HID - 128 * j)
                wbh, wbl = w2b[j % 2]
                P.step("sync", lambda e, j=j, m=m, wbh=wbh: e.dma_start(
                    out=wbh.ap[0:m, :], in_=w2sr[0][128 * j: 128 * j + m, :]),
                    reads=[], writes=[wbh], inc=16, dsem=w2sem[j % 2])
                P.step("sync", lambda e, j=j, m=m, wbl=wbl: e.dma_start(
                    out=wbl.ap[0:m, :], in_=w2sr[1][128 * j: 128 * j + m, :]),
                    reads=[], writes=[wbl], inc=16, dsem=w2sem[2 + j % 2])
                for dc in range(NCH):
                    triples = ((wbh, gg[0]), (wbh, gg[1]), (wbl, gg[0]))
                    for ti, (wb_, ggi) in enumerate(triples):
                        st = (j == 0 and ti == 0)
                        sp = (j == njtot - 1 and ti == 2)
                        P.step("tensor", lambda e, wb_=wb_, ggi=ggi, dc=dc, j=j, m=m,
                               st=st, sp=sp:
                               e.matmul(pool[dc].ap,
                                        wb_.ap[0:m, 128 * dc: 128 * (dc + 1)],
                                        ggi.ap[0:m, 512 * j: 512 * (j + 1)],
                                        start=st, stop=sp),
                               reads=[wb_, ggi], writes=[],
                               extra_deps=fd[dc] if st else [])
            for dc in range(NCH):
                pool[dc].w = ("tensor", P.cnt["tensor"])
                pool[dc].rs = []
                dst = stream[dc].ap[:, 1 + 512 * tb: 1 + 512 * (tb + 1)]
                P.step("vector", lambda e, dst=dst, dc=dc: e.scalar_tensor_tensor(
                    dst, pool[dc].ap, b2v.ap[:, dc:dc + 1], dst, OP.add, OP.add),
                    reads=[pool[dc], b2v, stream[dc]], writes=[stream[dc]])

        emit_ln2(0)
        for tb in range(4):
            if tb + 1 < 4:
                emit_ln2(tb + 1)
            emit_mlp_tb(tb)


def split_weights(nc, P, G, env, W, w1s, w2s, wqkvs, wofs):
    with ExitStack() as L:
        f32 = Buf(L.enter_context(nc.sbuf_tensor("sw_f32", [128, 5460], F32)).ap())
        hi = Buf(L.enter_context(nc.sbuf_tensor("sw_hi", [128, 5460], BF16)).ap())
        lo = Buf(L.enter_context(nc.sbuf_tensor("sw_lo", [128, 5460], BF16)).ap())
        insem = P.dma_sem("swin")
        outsem_h = P.dma_sem("swouth")
        outsem_l = P.dma_sem("swoutl")
        jobs = []
        for l in range(2):
            for dc in range(NCH):
                jobs.append((W["ffn_w1"][l][128 * dc:128 * (dc + 1), :],
                             w1s[l][0][128 * dc:128 * (dc + 1), :],
                             w1s[l][1][128 * dc:128 * (dc + 1), :], 5460, 128))
            for dc in range(NCH):
                jobs.append((W["ffn_wqkv"][l][128 * dc:128 * (dc + 1), :],
                             wqkvs[l][0][128 * dc:128 * (dc + 1), :],
                             wqkvs[l][1][128 * dc:128 * (dc + 1), :], 1536, 128))
            for j in range(ceil_div(HID, 128)):
                m = min(128, HID - 128 * j)
                jobs.append((W["ffn_w2"][l][128 * j:128 * j + m, :],
                             w2s[l][0][128 * j:128 * j + m, :],
                             w2s[l][1][128 * j:128 * j + m, :], D, m))
            for j in range(4):
                jobs.append((W["ffn_wo"][l][128 * j:128 * (j + 1), :],
                             wofs[l][0][128 * j:128 * (j + 1), :],
                             wofs[l][1][128 * j:128 * (j + 1), :], D, 128))
        for src_ap, hi_ap, lo_ap, n, m in jobs:
            P.step("sync", lambda e, src_ap=src_ap, n=n, m=m: e.dma_start(
                out=f32.ap[0:m, 0:n], in_=src_ap),
                reads=[], writes=[f32], inc=16, dsem=insem)
            P.step("vector", lambda e, n=n, m=m: e.tensor_copy(hi.ap[0:m, 0:n],
                                                               f32.ap[0:m, 0:n]),
                   reads=[f32], writes=[hi])
            P.step("vector", lambda e, n=n, m=m: e.tensor_tensor(
                lo.ap[0:m, 0:n], f32.ap[0:m, 0:n], hi.ap[0:m, 0:n], op=OP.subtract),
                reads=[f32, hi], writes=[lo])
            P.step("gpsimd", lambda e, hi_ap=hi_ap, n=n, m=m: e.dma_start(
                out=hi_ap, in_=hi.ap[0:m, 0:n]),
                reads=[hi], writes=[], inc=16, dsem=outsem_h)
            P.step("gpsimd", lambda e, lo_ap=lo_ap, n=n, m=m: e.dma_start(
                out=lo_ap, in_=lo.ap[0:m, 0:n]),
                reads=[lo], writes=[], inc=16, dsem=outsem_l)
    P.barrier()


def precompute_AB(nc, P, G, env, W, park_ab):
    ident = env["ident"]
    with ExitStack() as L:
        ts = lambda n, p, f: L.enter_context(nc.sbuf_tensor(f"AB_{n}", [p, f], F32))
        pool = [Buf(L.enter_context(
            nc.psum_tensor(f"AB_pp{i}", [128, 512], F32)).ap()) for i in range(4)]
        kup = Buf(ts("kup", 32, 128).ap())
        vup = Buf(ts("vup", 32, 128).ap())
        kupT = Buf(ts("kupT", 128, 32).ap())
        vupT = Buf(ts("vupT", 128, 32).ap())
        wqs = Buf(ts("wqs", 128, 8 * 128).ap())
        wqT = Buf(ts("wqT", 128, 8 * 128).ap())
        wos = Buf(ts("wos", 128, 1024).ap())
        At = Buf(ts("At", 32, 8 * 128).ap())
        Bt = Buf(ts("Bt", 32, 1024).ap())
        A_all = Buf(ts("A_all", 128, 2 * 8 * 128).ap())
        B_all = Buf(ts("B_all", 128, 2 * 1024).ap())
        absem = P.dma_sem("ABload")
        P.step("sync", lambda e: e.dma_start(out=kup.ap, in_=W["attn_wk_up"][:, :]),
               reads=[], writes=[kup], inc=16, dsem=absem)
        P.step("sync", lambda e: e.dma_start(out=vup.ap, in_=W["attn_wv_up"][:, :]),
               reads=[], writes=[vup], inc=16, dsem=absem)
        tok = (absem, P.dma_cnt[absem])
        kup.w = tok
        vup.w = tok
        for src, dst in ((kup, kupT), (vup, vupT)):
            P.step("tensor", lambda e, src=src: e.transpose(
                pool[0].ap[:, 0:32], src.ap, ident.ap[0:32, 0:32]),
                reads=[src, ident], writes=[pool[0]])
            P.step("vector", lambda e, dst=dst: e.tensor_copy(dst.ap, pool[0].ap[:, 0:32]),
                   reads=[pool[0]], writes=[dst])
        wqr = W["attn_wq"].rearrange("(a p) n -> p a n", p=128)
        wsem = P.dma_sem("ABwq")
        osem = P.dma_sem("ABwo")
        for h in range(MLA_HEADS):
            g, r = h // 4, h % 4
            P.step("sync", lambda e, h=h: e.dma_start(
                out=wqs.ap.rearrange("p (a n) -> p a n", n=128),
                in_=wqr[:, :, 128 * h:128 * (h + 1)]),
                reads=[], writes=[wqs], inc=16, dsem=wsem)
            for dc in range(NCH):
                pb = pool[dc % 2]
                P.step("tensor", lambda e, dc=dc, pb=pb: e.transpose(
                    pb.ap[:, 0:128], wqs.ap[:, 128 * dc:128 * (dc + 1)], ident.ap),
                    reads=[wqs, ident], writes=[pb])
                P.step("vector", lambda e, dc=dc, pb=pb: e.tensor_copy(
                    wqT.ap[:, 128 * dc:128 * (dc + 1)], pb.ap[:, 0:128]),
                    reads=[pb], writes=[wqT])
            for dc in range(NCH):
                pb = pool[2 + dc % 2]
                P.step("tensor", lambda e, dc=dc, pb=pb: e.matmul(
                    pb.ap[0:32, 0:128], kupT.ap, wqT.ap[:, 128 * dc:128 * (dc + 1)],
                    start=True, stop=True),
                    reads=[kupT, wqT], writes=[pb])
                P.step("vector", lambda e, dc=dc, pb=pb: e.tensor_scalar(
                    At.ap[:, 128 * dc:128 * (dc + 1)], pb.ap[0:32, 0:128],
                    MLA_SCALE, None, op0=OP.mult),
                    reads=[pb], writes=[At])
            for dc in range(NCH):
                pb = pool[dc % 2]
                P.step("tensor", lambda e, dc=dc, pb=pb: e.transpose(
                    pb.ap[:, 0:32], At.ap[:, 128 * dc:128 * (dc + 1)],
                    ident.ap[0:32, 0:32]),
                    reads=[At, ident], writes=[pb])
                col = (g * 8 + dc) * 128 + r * 32
                P.step("vector", lambda e, col=col, pb=pb: e.tensor_copy(
                    A_all.ap[:, col:col + 32], pb.ap[:, 0:32]),
                    reads=[pb], writes=[A_all])
            P.step("sync", lambda e, h=h: e.dma_start(
                out=wos.ap, in_=W["attn_wo"][128 * h:128 * (h + 1), :]),
                reads=[], writes=[wos], inc=16, dsem=osem)
            pb = pool[2]
            for half in range(2):
                mm_acc2(P, pb, [(vupT.ap, wos.ap[:, 512 * half: 512 * (half + 1)],
                                 [vupT, wos], pb.ap[0:32, :])])
                P.step("vector", lambda e, half=half, pb=pb: e.tensor_copy(
                    Bt.ap[:, 512 * half:512 * (half + 1)], pb.ap[0:32, :]),
                    reads=[pb], writes=[Bt])
            P.step("gpsimd", lambda e, g=g, r=r: e.dma_start(
                out=B_all.ap[32 * r:32 * (r + 1), 1024 * g:1024 * (g + 1)],
                in_=Bt.ap),
                reads=[Bt], writes=[B_all], inc=16, dsem=P.dma_sem("ABmv"))
        psem = P.dma_sem("ABpark")
        P.step("gpsimd", lambda e: e.dma_start(out=park_ab[0], in_=A_all.ap),
               reads=[A_all], writes=[], inc=16, dsem=psem)
        P.step("gpsimd", lambda e: e.dma_start(out=park_ab[1], in_=B_all.ap),
               reads=[B_all], writes=[], inc=16, dsem=psem)
    P.barrier()


def phase_C(nc, P, G, env, W, park_xn, park_ab, s):
    stream = env["stream"]
    onesm, triu, ident = env["onesm"], env["triu"], env["ident"]

    with ExitStack() as LC:
        ts = lambda n, p, f: LC.enter_context(nc.sbuf_tensor(f"C{s}_{n}", [p, f], F32))
        pool = [Buf(LC.enter_context(
            nc.psum_tensor(f"C{s}_pp{i}", [128, 512], F32)).ap()) for i in range(8)]
        vlat_tm = Buf(ts("vlat_tm", 128, 16 * 64).ap())
        tsb = lambda n, p, f: LC.enter_context(nc.sbuf_tensor(f"C{s}_{n}", [p, f], BF16))
        klh = [Buf(tsb(f"klh{g}", 128, T).ap()) for g in range(2)]
        kll = [Buf(tsb(f"kll{g}", 128, T).ap()) for g in range(2)]
        qth = [Buf(tsb(f"qth{g}", 128, T).ap()) for g in range(2)]
        qtl = [Buf(tsb(f"qtl{g}", 128, T).ap()) for g in range(2)]
        B_all = Buf(ts("B_all", 128, 2 * 1024).ap())
        absem2 = P.dma_sem("Cab")
        P.step("sync", lambda e: e.dma_start(out=B_all.ap, in_=park_ab[1]),
               reads=[], writes=[B_all], inc=16, dsem=absem2,
               extra_deps=[("ABpark", 32)])

        with ExitStack() as LS:
            ts2 = lambda n, p, f: LS.enter_context(
                nc.sbuf_tensor(f"C{s}s_{n}", [p, f], F32))
            xn_t = ts2("xn", 128, NCH * CW)
            xnc = [Buf(xn_t[:, c * CW + 1: c * CW + 1 + T]) for c in range(NCH)]
            A_all = Buf(ts2("A_all", 128, 2 * 8 * 128).ap())
            kdr = Buf(ts2("kdr", 128, 2 * 8 * 128).ap())
            vdn = Buf(ts2("vdn", 128, 8 * 64).ap())
            vlat_f = Buf(ts2("vlat_f", 64, T).ap())
            xsem = P.dma_sem("Cxn")
            ptok = (env["park_sem"], P.dma_cnt[env["park_sem"]])
            xtok = P.step("sync", lambda e: e.dma_start(out=xn_t[:, :], in_=park_xn[s]),
                          reads=[], writes=[], inc=16, dsem=xsem, extra_deps=[ptok])
            for b in xnc:
                b.w = xtok
            P.step("sync", lambda e: e.dma_start(out=A_all.ap, in_=park_ab[0]),
                   reads=[], writes=[A_all], inc=16, dsem=absem2,
                   extra_deps=[("ABpark", 32)])
            wdsem = P.dma_sem("Cwd")
            P.step("sync", lambda e: e.dma_start(out=kdr.ap, in_=W["attn_wkdr"][:, :]),
                   reads=[], writes=[kdr], inc=16, dsem=wdsem)
            P.step("sync", lambda e: e.dma_start(out=vdn.ap, in_=W["attn_wvdr"][:, :]),
                   reads=[], writes=[vdn], inc=16, dsem=wdsem)
            wdtok = (wdsem, P.dma_cnt[wdsem])
            kdr.w = wdtok
            vdn.w = wdtok
            for g in range(2):
                for tb in range(4):
                    pb = pool[tb % 2]
                    items = [(kdr.ap[:, (g * 8 + dc) * 128:(g * 8 + dc + 1) * 128],
                              xnc[dc].ap[:, 512 * tb:512 * (tb + 1)],
                              [kdr, xnc[dc]], None) for dc in range(NCH)]
                    mm_acc2(P, pb, items)
                    P.step("scalar", lambda e, g=g, tb=tb, pb=pb: e.copy(
                        klh[g].ap[:, 512 * tb:512 * (tb + 1)], pb.ap),
                        reads=[pb], writes=[klh[g]])
                    P.step("vector", lambda e, g=g, tb=tb, pb=pb: e.tensor_tensor(
                        kll[g].ap[:, 512 * tb:512 * (tb + 1)], pb.ap,
                        klh[g].ap[:, 512 * tb:512 * (tb + 1)], op=OP.subtract),
                        reads=[pb, klh[g]], writes=[kll[g]])
            for tb in range(4):
                pb = pool[2 + tb % 2]
                items = [(vdn.ap[:, dc * 64:(dc + 1) * 64],
                          xnc[dc].ap[:, 512 * tb:512 * (tb + 1)],
                          [vdn, xnc[dc]], pb.ap[0:64, :]) for dc in range(NCH)]
                mm_acc2(P, pb, items)
                P.step("scalar", lambda e, tb=tb, pb=pb: e.copy(
                    vlat_f.ap[:, 512 * tb:512 * (tb + 1)], pb.ap[0:64, :]),
                    reads=[pb], writes=[vlat_f])
            for kt in range(16):
                pb = pool[4 + kt % 2]
                P.step("tensor", lambda e, kt=kt, pb=pb: e.transpose(
                    pb.ap[:, 0:64], vlat_f.ap[:, 128 * kt:128 * (kt + 1)],
                    ident.ap[0:64, 0:64]),
                    reads=[vlat_f, ident], writes=[pb])
                P.step("vector", lambda e, kt=kt, pb=pb: e.tensor_copy(
                    vlat_tm.ap[:, 64 * kt:64 * (kt + 1)], pb.ap[:, 0:64]),
                    reads=[pb], writes=[vlat_tm])
            for g in range(2):
                for tb in range(4):
                    pb = pool[6 + tb % 2]
                    items = [(A_all.ap[:, (g * 8 + dc) * 128:(g * 8 + dc + 1) * 128],
                              xnc[dc].ap[:, 512 * tb:512 * (tb + 1)],
                              [A_all, xnc[dc]], None) for dc in range(NCH)]
                    mm_acc2(P, pb, items)
                    P.step("scalar", lambda e, g=g, tb=tb, pb=pb: e.copy(
                        qth[g].ap[:, 512 * tb:512 * (tb + 1)], pb.ap),
                        reads=[pb], writes=[qth[g]])
                    P.step("vector", lambda e, g=g, tb=tb, pb=pb: e.tensor_tensor(
                        qtl[g].ap[:, 512 * tb:512 * (tb + 1)], pb.ap,
                        qth[g].ap[:, 512 * tb:512 * (tb + 1)], op=OP.subtract),
                        reads=[pb, qth[g]], writes=[qtl[g]])
        P.barrier()
        probs = [Buf(ts(f"probs{p}", 128, T).ap()) for p in range(2)]
        probsT = [Buf(ts(f"probsT{p}", 128, T).ap()) for p in range(2)]
        ztile = [Buf(ts(f"ztile{p}", 128, 128).ap()) for p in range(2)]
        z_fm = [Buf(ts(f"zfm{g}", 128, T).ap()) for g in range(2)]
        nmax = [Buf(ts(f"nmax{p}", 128, 1).ap()) for p in range(2)]
        rsum = [Buf(ts(f"rsum{p}", 128, 1).ap()) for p in range(2)]
        rsum2 = [Buf(ts(f"rsum2{p}", 128, 1).ap()) for p in range(2)]

        rcnt = 0
        for g in range(2):
            for qi in range(16):
                for r in range(4):
                    h = g * 4 + r
                    par = rcnt % 2
                    rcnt += 1
                    Sp = pool[4 * par:4 * par + 4]
                    prb, prT = probs[par], probsT[par]
                    nmx, rsm, rs2 = nmax[par], rsum[par], rsum2[par]
                    zt = ztile[qi % 2]
                    nk = qi + 1
                    nblk = ceil_div(nk * 128, 512)
                    fd = []
                    for bb in range(nblk):
                        fd += ([Sp[bb].w] if Sp[bb].w else []) + Sp[bb].rs
                    fd = Prog.merge(fd + [qth[g].w, qtl[g].w, klh[g].w, kll[g].w])
                    first = True
                    for bkk in range(nblk):
                        n0 = 512 * bkk
                        n1 = min(nk * 128, n0 + 512)
                        qsl = slice(32 * r, 32 * (r + 1))
                        csl = slice(128 * qi, 128 * (qi + 1))
                        combos = ((qth[g], klh[g]), (qth[g], kll[g]), (qtl[g], klh[g]))
                        for ci, (qb, kb) in enumerate(combos):
                            P.step("tensor", lambda e, qb=qb, kb=kb, pb=Sp[bkk], n0=n0,
                                   n1=n1, r=r, qsl=qsl, csl=csl, st=(ci == 0),
                                   sp=(ci == 2): e.matmul(
                                pb.ap[:, 0:n1 - n0], qb.ap[qsl, csl],
                                kb.ap[qsl, n0:n1], start=st, stop=sp,
                                tile_position=(32 * r, 0)),
                                reads_reg=[qb, kb], writes=[],
                                extra_deps=fd if first else [])
                            first = False
                    for bkk in range(nblk):
                        Sp[bkk].w = ("tensor", P.cnt["tensor"])
                        Sp[bkk].rs = []
                    dbk = (nk - 1) * 128 // 512
                    doff = (nk - 1) * 128 - 512 * dbk
                    P.step("vector", lambda e, pb=Sp[dbk], doff=doff: e.tensor_tensor(
                        pb.ap[:, doff:doff + 128],
                        pb.ap[:, doff:doff + 128], triu.ap, op=OP.add),
                        reads=[Sp[dbk], triu], writes=[Sp[dbk]])
                    for bkk in range(nblk):
                        n0 = 512 * bkk
                        n1 = min(nk * 128, n0 + 512)
                        if bkk == 0:
                            P.step("vector", lambda e, pb=Sp[bkk], n=n1 - n0, nmx=nmx:
                                   e.reduce_max(
                                nmx.ap, pb.ap[:, 0:n], axis=AX.X, negate=True),
                                reads=[Sp[bkk], nmx], writes=[nmx])
                        else:
                            P.step("vector", lambda e, pb=Sp[bkk], n=n1 - n0, rs2=rs2:
                                   e.reduce_max(
                                rs2.ap, pb.ap[:, 0:n], axis=AX.X, negate=True),
                                reads=[Sp[bkk], rs2], writes=[rs2])
                            P.step("vector", lambda e, nmx=nmx, rs2=rs2: e.tensor_tensor(
                                nmx.ap, nmx.ap, rs2.ap, op=OP.min),
                                reads=[nmx, rs2], writes=[nmx])
                    for bkk in range(nblk):
                        n0 = 512 * bkk
                        n1 = min(nk * 128, n0 + 512)
                        P.step("scalar", lambda e, pb=Sp[bkk], n0=n0, n1=n1, prb=prb,
                               nmx=nmx, acc=(rsm if bkk == 0 else rs2): e.activation(
                            out=prb.ap[:, n0:n1], in_=pb.ap[:, 0:n1 - n0],
                            func=AF.Exp, bias=nmx.ap, accum_out=acc.ap),
                            reads=[Sp[bkk], nmx],
                            writes=[prb, rsm if bkk == 0 else rs2])
                        if bkk > 0:
                            P.step("vector", lambda e, rsm=rsm, rs2=rs2: e.tensor_tensor(
                                rsm.ap, rsm.ap, rs2.ap, op=OP.add),
                                reads=[rsm, rs2], writes=[rsm])

                    P.step("vector", lambda e, rsm=rsm: e.reciprocal(rsm.ap, rsm.ap),
                           reads=[rsm], writes=[rsm])
                    ntp = 0
                    tb_i = 0
                    pb = Sp[tb_i]
                    fdT = ([pb.w] if pb.w else []) + pb.rs
                    for kt in range(nk):
                        sl = ntp % 4
                        P.step("tensor", lambda e, kt=kt, sl=sl, pb=pb, prb=prb:
                               e.transpose(
                            pb.ap[:, 128 * sl:128 * (sl + 1)],
                            prb.ap[:, 128 * kt:128 * (kt + 1)], ident.ap),
                            reads=[prb, ident], writes=[],
                            extra_deps=fdT if ntp == 0 else [])
                        ntp += 1
                        if sl == 3 or kt == nk - 1:
                            pb.w = ("tensor", P.cnt["tensor"])
                            pb.rs = []
                            lo = 128 * (kt - sl)
                            P.step("scalar", lambda e, lo=lo, sl=sl, pb=pb, prT=prT:
                                   e.copy(
                                prT.ap[:, lo:lo + 128 * (sl + 1)],
                                pb.ap[:, 0:128 * (sl + 1)]),
                                reads=[pb], writes=[prT])
                            tb_i = 1 - tb_i
                            pb = Sp[tb_i]
                            fdT = ([pb.w] if pb.w else []) + pb.rs
                            ntp = 0
                    pz = Sp[2]
                    items = [(prT.ap[:, 128 * kt:128 * (kt + 1)],
                              vlat_tm.ap[:, 64 * kt + 32 * g: 64 * kt + 32 * (g + 1)],
                              [prT, vlat_tm], pz.ap[:, 0:32]) for kt in range(nk)]
                    mm_acc2(P, pz, items)
                    P.step("vector", lambda e, r=r, pz=pz, zt=zt, rsm=rsm: e.tensor_scalar(
                        zt.ap[:, 32 * r:32 * (r + 1)], pz.ap[:, 0:32], rsm.ap,
                        None, op0=OP.mult),
                        reads=[pz, rsm], writes=[zt])
                par = (rcnt - 1) % 2
                pb = pool[4 * par + 3]
                zt = ztile[qi % 2]
                P.step("tensor", lambda e, pb=pb, zt=zt: e.transpose(
                    pb.ap[:, 0:128], zt.ap, ident.ap),
                    reads=[zt, ident], writes=[pb])
                P.step("vector", lambda e, g=g, qi=qi, pb=pb: e.tensor_copy(
                    z_fm[g].ap[:, 128 * qi:128 * (qi + 1)], pb.ap[:, 0:128]),
                    reads=[pb], writes=[z_fm[g]])
        for dc in range(NCH):
            for tb in range(4):
                pb = pool[7 - (tb % 2)]
                items = [(B_all.ap[:, 1024 * g + 128 * dc: 1024 * g + 128 * (dc + 1)],
                          z_fm[g].ap[:, 512 * tb:512 * (tb + 1)],
                          [B_all, z_fm[g]], None) for g in range(2)]
                mm_acc2(P, pb, items)
                dst = stream[dc].ap[:, 1 + 512 * tb: 1 + 512 * (tb + 1)]
                P.step("vector", lambda e, dst=dst, pb=pb: e.tensor_tensor(
                    dst, dst, pb.ap, op=OP.add),
                    reads=[pb, stream[dc]], writes=[stream[dc]])


def mm_acc2(P, pb, items):
    fd = ([pb.w] if pb.w else []) + pb.rs
    for _, _, rd, _ in items:
        fd += [b.w for b in rd]
    fd = Prog.merge(fd)
    n = len(items)
    for i, (lhs, rhs, rd, osl) in enumerate(items):
        out = pb.ap if osl is None else osl
        P.step("tensor", lambda e, lhs=lhs, rhs=rhs, out=out, st=(i == 0),
               sp=(i == n - 1): e.matmul(out, lhs, rhs, start=st, stop=sp),
               reads_reg=rd, writes=[], extra_deps=fd if i == 0 else [])
    pb.w = ("tensor", P.cnt["tensor"])
    pb.rs = []


def phase_D(nc, P, G, env, x_in, out_t, s):
    stream = env["stream"]
    ident = env["ident"]
    with ExitStack() as L:
        ts = lambda n, p, f: L.enter_context(nc.sbuf_tensor(f"D{s}_{n}", [p, f], F32))
        xt = [Buf(ts(f"xt{i}", 128, D).ap()) for i in range(2)]
        ot = [Buf(ts(f"ot{i}", 128, D).ap()) for i in range(2)]
        pt = [Buf(L.enter_context(nc.psum_tensor(f"D{s}_pt{i}", [128, D], F32)).ap())
              for i in range(2)]
        if "dx_sems" not in env:
            env["dx_sems"] = [P.dma_sem("dx0"), P.dma_sem("dx1")]
            env["do_sems"] = [P.dma_sem("do0"), P.dma_sem("do1")]
        dx = env["dx_sems"]
        do = env["do_sems"]
        for i in range(16):
            b = i % 2
            P.step("sync", lambda e, i=i, b=b: e.dma_start(
                out=xt[b].ap, in_=x_in[s, 128 * i:128 * (i + 1), :]),
                reads=[], writes=[xt[b]], inc=16, dsem=dx[b])
            fd = ([pt[b].w] if pt[b].w else []) + pt[b].rs
            for dc in range(NCH):
                P.step("tensor", lambda e, i=i, dc=dc, b=b: e.transpose(
                    pt[b].ap[:, 128 * dc:128 * (dc + 1)],
                    stream[dc].ap[:, 1 + 128 * i: 1 + 128 * (i + 1)], ident.ap),
                    reads=[stream[dc], ident], writes=[],
                    extra_deps=fd if dc == 0 else [])
            pt[b].w = ("tensor", P.cnt["tensor"])
            pt[b].rs = []
            P.step("vector", lambda e, b=b: e.tensor_tensor(
                ot[b].ap, xt[b].ap, pt[b].ap, op=OP.add),
                reads=[xt[b], pt[b]], writes=[ot[b]])
            P.step("gpsimd", lambda e, i=i, b=b: e.dma_start(
                out=out_t[s, 128 * i:128 * (i + 1), :], in_=ot[b].ap),
                reads=[ot[b]], writes=[], inc=16, dsem=do[b])


def phase_A(nc, P, G, env, x_in, park_xn, s):
    stream = env["stream"]
    stream_t = env["stream_t"]
    ident = env["ident"]
    with ExitStack() as L:
        ts = lambda n, p, f, dt=F32: L.enter_context(nc.sbuf_tensor(f"A{s}_{n}", [p, f], dt))
        xt = [Buf(ts(f"xt{i}", 128, D).ap()) for i in range(2)]
        if "xa_sems" not in env:
            env["xa_sems"] = [P.dma_sem("xa0"), P.dma_sem("xa1")]
            env["park_sem"] = P.dma_sem("park")
        xa = env["xa_sems"]
        xn = [Buf(ts(f"xn{i}", 128, D).ap()) for i in range(2)]
        st6 = [Buf(ts(f"st6{i}", 128, 12).ap()) for i in range(2)]
        smv = [Buf(ts(f"smv{i}", 128, 2).ap()) for i in range(2)]
        rst = [Buf(ts(f"rst{i}", 128, 1).ap()) for i in range(2)]
        pt = [Buf(L.enter_context(nc.psum_tensor(f"A{s}_pt{i}", [128, D], F32)).ap())
              for i in range(2)]

        eps_v = Buf(ts("eps_v", 128, 1).ap())
        P.step("vector", lambda e: e.memset(eps_v.ap, 1e-5), reads=[eps_v], writes=[eps_v])
        zcols = stream_t[:, :].rearrange("p (c w) -> p c w", w=CW)[:, :, 0:1]
        ztok = P.em("vector", lambda e: e.memset(zcols, 0.0),
                    [b.w for b in stream] + [r for b in stream for r in b.rs], 1)
        for b in stream:
            b.w = ztok

        for i in range(16):
            b = i % 2
            x_src = x_in[s, 128 * i:128 * (i + 1), :]
            P.step("sync", lambda e, x_src=x_src, b=b: e.dma_start(out=xt[b].ap, in_=x_src),
                   reads=[], writes=[xt[b]], inc=16, dsem=xa[b])
            P.step("vector", lambda e, b=b: e.bn_stats(st6[b].ap[:, 0:6],
                                                       xt[b].ap[:, 0:512]),
                   reads=[xt[b], st6[b]], writes=[st6[b]])
            P.step("vector", lambda e, b=b: e.bn_stats(st6[b].ap[:, 6:12],
                                                       xt[b].ap[:, 512:1024]),
                   reads=[xt[b], st6[b]], writes=[st6[b]])
            P.step("vector", lambda e, b=b: e.bn_aggr(smv[b].ap, st6[b].ap),
                   reads=[st6[b], smv[b]], writes=[smv[b]])
            P.step("scalar", lambda e, b=b: e.activation(out=rst[b].ap,
                                                         in_=smv[b].ap[:, 1:2],
                                                         func=AF.Sqrt,
                                                         bias=eps_v.ap),
                   reads=[smv[b], rst[b]], writes=[rst[b]])
            P.step("vector", lambda e, b=b: e.reciprocal(rst[b].ap, rst[b].ap),
                   reads=[rst[b]], writes=[rst[b]])
            P.step("vector", lambda e, b=b: e.tensor_scalar(xn[b].ap, xt[b].ap,
                                                            smv[b].ap[:, 0:1],
                                                            rst[b].ap, op0=OP.subtract,
                                                            op1=OP.mult),
                   reads=[xt[b], smv[b], rst[b]], writes=[xn[b]])
            first_deps = ([pt[b].w] if pt[b].w else []) + pt[b].rs
            for c in range(NCH):
                tok = P.step("tensor", lambda e, b=b, c=c: e.transpose(
                    pt[b].ap[:, 128 * c:128 * (c + 1)], xn[b].ap[:, 128 * c:128 * (c + 1)],
                    ident.ap),
                    reads=[xn[b], ident], writes=[],
                    extra_deps=first_deps if c == 0 else [])
            pt[b].w = tok
            pt[b].rs = []
            dst = stream_t[:, :].rearrange("p (c w) -> p c w", w=CW)[:, :, 1 + 128 * i: 1 + 128 * (i + 1)]
            src = pt[b].ap.rearrange("p (c w) -> p c w", w=128)
            P.step("vector", lambda e, dst=dst, src=src: e.tensor_copy(dst, src),
                   reads=[pt[b]] + stream, writes=stream)
        tok = P.step("gpsimd", lambda e: e.dma_start(out=park_xn[s], in_=stream_t[:, :]),
                     reads=stream, writes=[], inc=16, dsem=env["park_sem"])
        env["park_tok"] = tok



def host_constants():
    ident = np.eye(128, dtype=np.float32)
    P64 = np.zeros((64, 64), np.float32)
    for d in range(32):
        P64[d, d + 32] = -1.0
        P64[d + 32, d] = 1.0
    Pm = np.zeros((128, 128), np.float32)
    Pm[0:64, 0:64] = P64
    Pm[64:128, 64:128] = P64
    protT = Pm.T.copy()
    idx = np.arange(128)
    tri_up = np.where(idx[:, None] < idx[None, :], np.float32(NEG), np.float32(0.0)).astype(np.float32)
    tri_lo = np.where(idx[:, None] > idx[None, :], np.float32(NEG), np.float32(0.0)).astype(np.float32)
    ind2 = np.zeros((2, 128), np.float32)
    ind2[0, 0:64] = 1.0
    ind2[1, 64:128] = 1.0
    dprime = (np.arange(128) % 32).astype(np.float64)
    invf64 = 10000.0 ** (-2.0 * dprime / 64.0)
    base64 = (2.0 * dprime + 0.4 * 64.0) / (1.4 * 64.0)
    lnb64 = np.log(base64)
    def hilo(v64):
        hi = v64.astype(np.float32)
        lo = (v64 - hi.astype(np.float64)).astype(np.float32)
        return hi, lo
    ih, il = hilo(invf64)
    lh, ll = hilo(lnb64)
    tabc = np.stack([ih, il, lh, ll]).reshape(4, 128, 1).astype(np.float32)
    return {"identc": ident, "protT": protT, "tri_up": tri_up, "tri_lo": tri_lo,
            "ind2c": ind2, "tabc": tabc}


def host_relayouts(inputs):
    out = {}
    qs = np.asarray(inputs["L0_ffn_q_scale"])
    ks = np.asarray(inputs["L0_ffn_k_scale"])
    idx = np.arange(128) % 64
    out["ffn_qsr"] = np.ascontiguousarray(qs[:, idx][:, :, None])
    out["ffn_ksr"] = np.ascontiguousarray(ks[:, idx][:, :, None])
    b1 = np.asarray(inputs["L0_ffn_b1"])
    b1a = np.zeros((2, 22 * 128), np.float32)
    b1a[:, :HID] = b1[:, :HID]
    b1g = np.zeros((2, 22 * 128), np.float32)
    b1g[:, :HID] = b1[:, HID:]
    out["ffn_b1ar"] = np.ascontiguousarray(
        b1a.reshape(2, 22, 128).transpose(0, 2, 1))
    out["ffn_b1gr"] = np.ascontiguousarray(
        b1g.reshape(2, 22, 128).transpose(0, 2, 1))
    for key, name in [("L0_ffn_ln_g", "ffn_lngr"), ("L0_ffn_ln_b", "ffn_lnbr"),
                      ("L0_ffn_b2", "ffn_b2r")]:
        v = np.asarray(inputs[key])
        out[name] = np.ascontiguousarray(v.reshape(2, 8, 128).transpose(0, 2, 1))
    wkd = np.asarray(inputs["L0_attn_wk_down"])
    kdr = np.zeros((128, 2048), np.float32)
    for g in range(2):
        for dc in range(8):
            tile = np.zeros((128, 128), np.float32)
            for r in range(4):
                tile[:, 32 * r:32 * (r + 1)] = wkd[128 * dc:128 * (dc + 1),
                                                   32 * g:32 * (g + 1)]
            kdr[:, (g * 8 + dc) * 128:(g * 8 + dc + 1) * 128] = tile
    out["attn_wkdr"] = kdr
    wvd = np.asarray(inputs["L0_attn_wv_down"])
    vdr = np.zeros((128, 512), np.float32)
    for dc in range(8):
        vdr[:, 64 * dc:64 * (dc + 1)] = wvd[128 * dc:128 * (dc + 1), :]
    out["attn_wvdr"] = vdr
    return out


def kernel(**inputs):
    x = np.ascontiguousarray(inputs["L0_attn_x"])
    nc, _ = build_program()
    consts = host_constants()
    relay = host_relayouts(inputs)
    base = {}
    for k, v in inputs.items():
        if k == "L0_attn_x":
            continue
        base[k[3:]] = np.ascontiguousarray(v)
    base.update(consts)
    base.update(relay)
    in_maps = []
    for core in range(8):
        m = dict(base)
        m["x"] = x[core * NSEQ:(core + 1) * NSEQ]
        in_maps.append(m)
    res = run_bass_kernel_spmd(nc, in_maps, core_ids=list(range(8)))
    return np.concatenate([r["out"] for r in res.results], axis=0)
